# version 33
# baseline (speedup 1.0000x reference)
import numpy as np
from contextlib import ExitStack
import concourse.bass as bass
import concourse.mybir as mybir
from concourse.bass_utils import run_bass_kernel_spmd

F32 = mybir.dt.float32
BF16 = mybir.dt.bfloat16
FP8 = mybir.dt.float8e4
AF = mybir.ActivationFunctionType
ALU = mybir.AluOpType
AX = mybir.AxisListType

B = 16
NCORES = 8
SEQ = B // NCORES
T = 2048
D = 1024
HD = 64
NH = 16
KVH = 4
HF = 4096
SINK = 4
WIN = 256
EPS = 1e-5
C1 = float(3 * 2 ** 22)
C3 = float(3 * 2 ** 51)
NEG = float(np.finfo(np.float32).min)
TINY = float(np.finfo(np.float32).tiny)
TT = T // 128
QW = 512
QWB = 256


class _Step:
    __slots__ = ("eng", "cum")

    def __init__(self, eng, cum):
        self.eng, self.cum = eng, cum


class Buf:
    __slots__ = ("w", "r", "name")

    def __init__(self, name=""):
        self.w, self.r, self.name = None, [], name


class Sched:
    DMA = ("sdma", "gdma")

    def __init__(self, nc):
        self.nc = nc
        self.order = []
        self.cum = {e: 0 for e in ("pe", "act", "dve", "gp", "sdma", "gdma")}
        self.fence = []
        self.last_dma = {"sdma": None, "gdma": None}

    def do(self, eng, fn, reads=(), writes=(), extra=()):
        deps = list(self.fence)
        for b in reads:
            if b.w is not None:
                deps.append(b.w)
        for b in writes:
            if b.w is not None:
                deps.append(b.w)
            deps.extend(b.r)
        deps.extend(x for x in extra if x is not None)
        if eng in self.DMA and self.last_dma[eng] is not None:
            deps.append(self.last_dma[eng])
        inc = 16 if eng in self.DMA else 1
        self.cum[eng] += inc
        st = _Step(eng, self.cum[eng])
        self.order.append((st, fn, deps))
        for b in reads:
            b.r.append(st)
        for b in writes:
            b.w = st
            b.r = []
        if eng in self.DMA:
            self.last_dma[eng] = st
        return st

    def set_fence(self):
        last = {}
        for st, _, _ in self.order:
            last[st.eng] = st
        self.fence = list(last.values())

    def emit(self):
        nc = self.nc
        plans = {e: [] for e in ("pe", "act", "dve", "gpsimd", "sdma")}
        for st, fn, deps in self.order:
            need = {}
            for d in deps:
                if d.eng == st.eng == "pe":
                    continue
                if d.cum > need.get(d.eng, 0):
                    need[d.eng] = d.cum
            stream = {"gp": "gpsimd", "gdma": "gpsimd"}.get(st.eng, st.eng)
            plans[stream].append((st, fn, need))

        with ExitStack() as es:
            sems = {e: es.enter_context(nc.semaphore(f"sem_{e}"))
                    for e in ("pe", "act", "dve", "gp", "sdma", "gdma")}
            block = es.enter_context(nc.Block())
            waited = {s: {e: 0 for e in sems} for s in plans}

            def run(eng_obj, stream):
                for st, fn, need in plans[stream]:
                    for e in ("pe", "act", "dve", "gp", "sdma", "gdma"):
                        v = need.get(e, 0)
                        if waited[stream][e] < v:
                            eng_obj.wait_ge(sems[e], v)
                            waited[stream][e] = v
                    inst = fn(eng_obj)
                    if isinstance(inst, (list, tuple)):
                        inst = inst[-1]
                    inst.then_inc(sems[st.eng], 16 if st.eng in self.DMA else 1)

            if plans["sdma"]:
                @block.sync
                def _(sync):
                    run(sync, "sdma")
            if plans["gpsimd"]:
                @block.gpsimd
                def _(g):
                    run(g, "gpsimd")
            if plans["pe"]:
                @block.tensor
                def _(t):
                    run(t, "pe")
            if plans["act"]:
                @block.scalar
                def _(sc):
                    run(sc, "act")
            if plans["dve"]:
                @block.vector
                def _(v):
                    run(v, "dve")


def build_program(debug=False, trunc=None):
    nc = bass.Bass("TRN2", target_bir_lowering=False)
    sch = Sched(nc)

    x_in = nc.dram_tensor("x", [SEQ, T, D], F32, kind="ExternalInput").ap()
    out_d = nc.dram_tensor("out", [SEQ, T, D], F32, kind="ExternalOutput").ap()
    W = {}
    for nm, shp in [
        ("aWq", [D, D]), ("aWk", [D, D]), ("aWv", [D, D]), ("aWo", [D, D]),
        ("abo", [D]), ("bWq", [D, D]), ("bWk", [D, KVH * HD]),
        ("bWv", [D, KVH * HD]), ("bWo", [D, D]), ("fw1", [HF, D]),
        ("fb1", [HF]), ("fln1w", [D]), ("fln1b", [D]), ("fln2w", [HF]),
        ("fln2b", [HF]), ("fw2", [D, HF]), ("fb2", [D]),
    ]:
        W[nm] = nc.dram_tensor(nm, shp, F32, kind="ExternalInput").ap()
    kd = "ExternalOutput" if debug else "Internal"
    xs_a = nc.dram_tensor("xs_a", [SEQ, T, D], F32, kind=kd).ap()
    xs_b = nc.dram_tensor("xs_b", [SEQ, T, D], F32, kind=kd).ap()
    xs_c = nc.dram_tensor("xs_c", [SEQ, T, D], F32, kind=kd).ap()
    w1q_sp = nc.dram_tensor("w1q_sp", [128, 8 * HF], FP8, kind="Internal").ap()
    w2q_sp = nc.dram_tensor("w2q_sp", [128, 32 * D], FP8, kind="Internal").ap()

    CB = nc.alloc_sbuf_tensor("CB", [128, 16], F32)
    cb = Buf("CB")
    STATS = nc.alloc_sbuf_tensor("STATS", [128, 32], F32)
    stbs = [Buf("STATS0"), Buf("STATS1")]
    STATS2 = nc.alloc_sbuf_tensor("STATS2", [128, 48], F32)
    stb2s = [Buf(f"S2_{i}") for i in range(48)]
    SQSCR = nc.alloc_sbuf_tensor("SQSCR", [128, 1024], F32)
    sqbs = [Buf("SQ0"), Buf("SQ1")]
    ID32 = nc.alloc_sbuf_tensor("ID32", [128, 128], F32)
    IDBF = nc.alloc_sbuf_tensor("IDBF", [128, 128], BF16)
    ONES1 = nc.alloc_sbuf_tensor("ONES1", [128, 1], F32)
    ONESR = nc.alloc_sbuf_tensor("ONESR", [1, 128], F32)
    ONES1B = nc.alloc_sbuf_tensor("ONES1B", [1, 128], BF16)
    SELA = nc.alloc_sbuf_tensor("SELA", [1, 128], F32)
    SELB = nc.alloc_sbuf_tensor("SELB", [1, 128], F32)
    idn = Buf("ID")
    QS = nc.alloc_sbuf_tensor("QS", [128, 4], F32)
    qs = Buf("QS")

    (S_DISC, S_B1, S_APN, S_BPN, S_KNN, S_C1, S_EPS, S_SCALE, S_TINY,
     S_T0, S_T1, S_T2, S_C3) = range(13)

    def c(i):
        return CB[:, i:i + 1]

    def setup_consts():
        for slot, val in ((S_T0, float(2 ** 24)), (S_EPS, EPS),
                          (S_SCALE, 0.125), (S_TINY, TINY), (S_C1, C1),
                          (S_C3, C3)):
            sch.do("gp", lambda e, s=slot, v=val: e.memset(c(s), v), writes=[cb])
        sch.do("dve", lambda e: e.tensor_single_scalar(c(S_T1), c(S_T0), 1.0,
                                                       ALU.add),
               reads=[cb], writes=[cb])
        sch.do("dve", lambda e: e.tensor_tensor(c(S_DISC), c(S_T1), c(S_T0),
                                                ALU.is_equal),
               reads=[cb], writes=[cb])
        sch.do("dve", lambda e: e.tensor_scalar(c(S_BPN), c(S_DISC), 1.0, -1.0,
                                                ALU.mult, ALU.add),
               reads=[cb], writes=[cb])
        sch.do("dve", lambda e: e.tensor_single_scalar(c(S_APN), c(S_DISC),
                                                       -1.0, ALU.mult),
               reads=[cb], writes=[cb])
        sch.do("dve", lambda e: e.tensor_single_scalar(c(S_T0), c(S_DISC), C1,
                                                       ALU.mult),
               reads=[cb], writes=[cb])
        sch.do("dve", lambda e: e.tensor_scalar(c(S_T1), c(S_BPN), -C3, 0.0,
                                                ALU.mult, ALU.add),
               reads=[cb], writes=[cb])
        sch.do("dve", lambda e: e.tensor_tensor(c(S_KNN), c(S_T0), c(S_T1),
                                                ALU.add),
               reads=[cb], writes=[cb])
        sch.do("gp", lambda e: e.memset(ID32[:], 1.0), writes=[idn])
        sch.do("gp", lambda e: e.affine_select(
            ID32[:], ID32[:], [[1, 128]], ALU.is_equal, 0.0,
            base=0, channel_multiplier=-1), reads=[idn], writes=[idn])
        sch.do("dve", lambda e: e.tensor_copy(IDBF[:], ID32[:]),
               reads=[idn], writes=[idn])
        sch.do("gp", lambda e: e.memset(ONES1[:], 1.0), writes=[idn])
        sch.do("gp", lambda e: e.memset(ONESR[:], 1.0), writes=[idn])
        sch.do("gp", lambda e: e.memset(ONES1B[:], 1.0), writes=[idn])
        sch.do("gp", lambda e: e.memset(SELA[:], 1.0), writes=[idn])
        sch.do("gp", lambda e: e.memset(SELB[:], 1.0), writes=[idn])
        sch.do("gp", lambda e: e.affine_select(
            SELA[:], SELA[:], [[-1, 128]], ALU.is_ge, 0.0,
            base=63, channel_multiplier=0), reads=[idn], writes=[idn])
        sch.do("gp", lambda e: e.affine_select(
            SELB[:], SELB[:], [[1, 128]], ALU.is_ge, 0.0,
            base=-64, channel_multiplier=0), reads=[idn], writes=[idn])

    setup_consts()

    PS = [nc.alloc_psum_tensor(f"PS{i}", [128, 512], F32) for i in range(7)]
    psb = [Buf(f"PS{i}") for i in range(7)]
    PT = nc.alloc_psum_tensor("PT", [128, 1024], BF16)
    ptb = Buf("PT")

    taps = {}

    def tap(name, ap, shape, dt, reads):
        if not debug or name in taps:
            return
        h = nc.dram_tensor("dbg_" + name, shape, dt, kind="ExternalOutput").ap()
        taps[name] = True
        sch.do("gdma", lambda e: e.dma_start(out=h, in_=ap), reads=reads)

    def ln_stats(x_ap, x_buf, ncols, par, x_bufs=None):
        b0 = 16 * par
        stb = stbs[par]

        def S(i):
            return STATS[:, b0 + i:b0 + i + 1]

        sch.do("dve", lambda e: e.reduce_sum(S(0), x_ap, axis=AX.X),
               reads=([x_buf] if x_bufs is None else x_bufs), writes=[stb])
        nsl = ncols // 512
        a0 = 16 * par
        sch.do("dve", lambda e, nsl=nsl, a0=a0: e.memset(
            STATS2[:, a0:a0 + nsl], 0.0),
            writes=[stb2s[a0 + s] for s in range(nsl)])
        for s in range(nsl):
            sch.do("act", lambda e, s=s, a0=a0: e.activation(
                SQSCR[:, (s % 2) * 512:(s % 2) * 512 + 512],
                x_ap[:, s * 512:(s + 1) * 512], AF.Square,
                accum_out=STATS2[:, a0 + s:a0 + s + 1]),
                reads=[x_buf if x_bufs is None else x_bufs[s]],
                writes=[sqbs[s % 2], stb2s[a0 + s]])
        sch.do("dve", lambda e, nsl=nsl, a0=a0: e.reduce_sum(
            S(1), STATS2[:, a0:a0 + nsl], axis=AX.X),
            reads=[stb2s[a0 + s] for s in range(nsl)], writes=[stb])
        inv = 1.0 / ncols
        sch.do("dve", lambda e: e.tensor_single_scalar(
            S(0), S(0), inv, ALU.mult), reads=[stb], writes=[stb])
        sch.do("dve", lambda e: e.tensor_tensor(
            S(2), S(0), S(0), ALU.mult), reads=[stb], writes=[stb])
        sch.do("dve", lambda e: e.tensor_scalar(
            S(1), S(1), inv, 0.0, ALU.mult, ALU.add), reads=[stb], writes=[stb])
        sch.do("dve", lambda e: e.tensor_tensor(
            S(1), S(1), S(2), ALU.subtract), reads=[stb], writes=[stb])
        sch.do("act", lambda e: e.activation(S(6), S(1), AF.Sqrt,
                                             bias=c(S_EPS), scale=1.0),
               reads=[stb, cb], writes=[stb])
        sch.do("dve", lambda e: e.reciprocal(S(3), S(6)),
               reads=[stb], writes=[stb])
        sch.do("dve", lambda e: e.tensor_tensor(S(6), S(3), S(3), ALU.mult),
               reads=[stb], writes=[stb])
        sch.do("dve", lambda e: e.tensor_single_scalar(S(7), S(1), EPS,
                                                       ALU.add),
               reads=[stb], writes=[stb])
        sch.do("dve", lambda e: e.tensor_tensor(S(6), S(6), S(7), ALU.mult),
               reads=[stb], writes=[stb])
        sch.do("dve", lambda e: e.tensor_scalar(S(6), S(6), -0.5, 1.5,
                                                ALU.mult, ALU.add),
               reads=[stb], writes=[stb])
        sch.do("dve", lambda e: e.tensor_tensor(S(3), S(3), S(6), ALU.mult),
               reads=[stb], writes=[stb])

    def split_hilo(hi_ap, lo_ap, src_ap, src_buf, dst_buf):
        sch.do("act", lambda e: e.activation(hi_ap, src_ap, AF.Copy),
               reads=[src_buf], writes=[dst_buf])
        sch.do("dve", lambda e: e.scalar_tensor_tensor(
            lo_ap, src_ap, 0.0, hi_ap, ALU.add, ALU.subtract),
            reads=[src_buf, dst_buf], writes=[dst_buf])

    def attention_phase(kind, src_dram, dst_dram, seq, tag):
        es = ExitStack()
        sb = lambda n, shp, dt: es.enter_context(nc.sbuf_tensor(tag + n, shp, dt))
        XT = sb("XT", [128, TT * D], F32)
        xtb = [Buf(f"XT{i}") for i in range(TT)]
        YHI = sb("YHI", [128, 8 * T], BF16)
        YLO = sb("YLO", [128, 8 * T], BF16)
        ybs = [Buf(f"Y{i}") for i in range(TT)]
        KHI = sb("KHI", [128, T], BF16)
        KLO = sb("KLO", [128, T], BF16)
        kb = Buf("K")
        VEXA = sb("VEXA", [128, 16 * 66], F32)
        VEXB = sb("VEXB", [128, 16 * 66], F32)
        vb = Buf("V")
        ONT = sb("ONT", [128, QW], F32)
        RCQ = sb("RCQ", [128, 2], F32)
        rcq = Buf("RCQ")
        VSTG = sb("VSTG", [128, 512], F32)
        vsb = Buf("VS")
        QHI = sb("QHI", [128, QW], BF16)
        QLO = sb("QLO", [128, QW], BF16)
        qb = Buf("Q")
        ETW = QW if kind == "a" else QWB
        NE = 3 if kind == "a" else 4
        ET = sb("ET", [128, NE * ETW], F32)
        eb = [Buf(f"E{i}") for i in range(NE)]
        ONR = sb("ONR", [128, QW], F32)
        ONHI = sb("ONHI", [128, QW], BF16)
        ONLO = sb("ONLO", [128, QW], BF16)
        onb = Buf("ON")
        RCPA = sb("RCPA", [1, 512], F32)
        RCPB_ = sb("RCPB_", [1, 512], F32)
        rcb = Buf("RCP")
        RCPB = sb("RCPBT", [128, 512], F32)
        rcpb = Buf("RCPB")
        WSTG = sb("WSTG", [128, 1024], F32)
        wsb = Buf("WS")
        WQH = sb("WQH", [128, 1024], BF16)
        WQL = sb("WQL", [128, 1024], BF16)
        wqb = Buf("WQ")
        WKH = sb("WKH", [128, 1024], BF16)
        WKL = sb("WKL", [128, 1024], BF16)
        wkb = Buf("WK")
        WVH, WVL, wvb = WKH, WKL, wkb
        WOH = sb("WOH", [128, 1024], BF16)
        WOL = sb("WOL", [128, 1024], BF16)
        wob = Buf("WO")
        if kind == "a":
            BORH = sb("BORH", [1, 1024], BF16)
            BORL = sb("BORL", [1, 1024], BF16)
        bob = Buf("BO")
        YTMP = sb("YTMP", [128, 2 * D], F32)
        ytbs = [Buf("YT0"), Buf("YT1")]
        if kind == "b":
            MSK = sb("MSK", [128, 6 * QWB], BF16)
            mkb = Buf("MSK")
        sch.set_fence()

        sch.do("gp", lambda e: e.memset(VEXA[:, :], 0.0), writes=[vb])
        sch.do("gp", lambda e: e.memset(VEXB[:, :], 0.0), writes=[vb])
        sch.do("gp", lambda e: e.memset(
            VEXA[:, :].rearrange("p (c n) -> p c n", n=66)[:, :, 64:65], 1.0),
            reads=[vb], writes=[vb])
        sch.do("gp", lambda e: e.memset(
            VEXB[:, :].rearrange("p (c n) -> p c n", n=66)[:, :, 64:65], 1.0),
            reads=[vb], writes=[vb])
        qw = QW if kind == "a" else QWB
        nqt = T // qw
        Wq = W["aWq"] if kind == "a" else W["bWq"]
        Wk = W["aWk"] if kind == "a" else W["bWk"]
        Wv = W["aWv"] if kind == "a" else W["bWv"]
        Wo = W["aWo"] if kind == "a" else W["bWo"]

        def mask_t(i):
            return MSK[:, i * QWB:(i + 1) * QWB]

        if kind == "b":
            specs = [(0, 1, -128, -1), (1, 1, 0, -1),
                     (2, -1, 127, 1), (3, -1, -1, 1)]
            for idx, fc, base, cm in specs:
                sch.do("gp", lambda e, i=idx: e.memset(mask_t(i), 0.0),
                       writes=[mkb])
                sch.do("gp", lambda e, i=idx, fc=fc, base=base, cm=cm:
                       e.affine_select(mask_t(i), mask_t(i), [[fc, QWB]],
                                       ALU.is_ge, -1e38, base=base,
                                       channel_multiplier=cm),
                       reads=[mkb], writes=[mkb])
            for src, dst in ((2, 4), (3, 5)):
                sch.do("gp", lambda e, s=src, d=dst: e.tensor_copy(
                    mask_t(d), mask_t(s)), reads=[mkb], writes=[mkb])
                sch.do("gp", lambda e, d=dst: e.memset(
                    mask_t(d)[0:SINK, :], 0.0), reads=[mkb], writes=[mkb])

        def load_w_cols(dram, col0, ncol, replicate, hi_t, lo_t, dbuf):
            n = 2 * ncol if replicate else ncol
            src = dram[:, col0:col0 + ncol].rearrange(
                "(c p) n -> p c n", p=128)
            outv = WSTG[:, 0:8 * n].rearrange("p (c n) -> p c n", c=8)
            if replicate:
                sch.do("sdma", lambda e: e.dma_start(
                    out=outv[:, :, 0:ncol], in_=src), writes=[wsb])
                sch.do("sdma", lambda e: e.dma_start(
                    out=outv[:, :, ncol:2 * ncol], in_=src),
                    reads=[wsb], writes=[wsb])
            else:
                sch.do("sdma", lambda e: e.dma_start(out=outv, in_=src),
                       writes=[wsb])
            split_hilo(hi_t[:, 0:8 * n], lo_t[:, 0:8 * n], WSTG[:, 0:8 * n],
                       wsb, dbuf)

        for i in range(TT):
            sch.do("gdma", lambda e, i=i, seq=seq: e.dma_start(
                out=XT[:, i * D:(i + 1) * D],
                in_=src_dram[seq, i * 128:(i + 1) * 128, :]), writes=[xtb[i]])
            par = i % 2
            yt = YTMP[:, par * D:(par + 1) * D]
            ln_stats(XT[:, i * D:(i + 1) * D], xtb[i], D, par)
            sch.do("dve", lambda e, i=i, par=par, yt=yt: e.tensor_scalar(
                yt, XT[:, i * D:(i + 1) * D],
                STATS[:, 16 * par:16 * par + 1],
                STATS[:, 16 * par + 3:16 * par + 4],
                ALU.subtract, ALU.mult),
                reads=[xtb[i], stbs[par]], writes=[ytbs[par]])
            if tag == "A0_" and i == 0:
                tap("xt0", XT[:, 0:D], [128, D], F32, [xtb[0]])
                tap("ytmp0", YTMP[:, 0:D], [128, D], F32, [ytbs[0]])
                tap("stats0", STATS[:, 0:4], [128, 4], F32, [stbs[0]])
            for half in range(2):
                pb, pst = psb[half], PS[half]
                def tp(e, half=half, pst=pst, yt=yt):
                    ins = []
                    for j in range(4):
                        ch = half * 4 + j
                        ins.append(e.matmul(
                            pst[:, j * 128:(j + 1) * 128],
                            yt[:, ch * 128:(ch + 1) * 128], ID32[:],
                            start=(j == 0), stop=(j == 3), is_transpose=True))
                    return ins
                sch.do("pe", tp, reads=[ytbs[par], idn], writes=[pb])
                yh = YHI[:, :].rearrange("p (c t) -> p c t", c=8)[
                    :, half * 4:half * 4 + 4, i * 128:(i + 1) * 128]
                yl = YLO[:, :].rearrange("p (c t) -> p c t", c=8)[
                    :, half * 4:half * 4 + 4, i * 128:(i + 1) * 128]
                pv = pst[:, :].rearrange("p (c t) -> p c t", c=4)
                split_hilo(yh, yl, pv, pb, ybs[i])

        if tag == "A0_":
            tap("yhi", YHI[:, :], [128, 8 * T], BF16, list(ybs))
            tap("ylo", YLO[:, :], [128, 8 * T], BF16, list(ybs))
        if trunc == "phaseA":
            es.close()
            sch.set_fence()
            return
        pend_tails = []

        def flush_tails():
            while pend_tails:
                pend_tails.pop(0)()

        for hp in range(NH // 2):
            flush_tails()
            if kind == "a":
                build_kv = True
                kcol = hp * 128
                vcol = hp * 128
                repl = False
            else:
                g = hp // 2
                build_kv = (hp % 2 == 0)
                kcol = g * 64
                vcol = g * 64
                repl = True
            if build_kv:
                load_w_cols(Wk, kcol, 64 if repl else 128, repl, WKH, WKL, wkb)
                for kt in range(T // 512):
                    pb, pst = psb[4 + (kt % 2)], PS[4 + (kt % 2)]
                    def mk(e, kt=kt, pst=pst):
                        ins = []
                        for ch in range(8):
                            a_hi = WKH[:, ch * 128:(ch + 1) * 128]
                            a_lo = WKL[:, ch * 128:(ch + 1) * 128]
                            b_hi = YHI[:, ch * T + kt * 512:
                                       ch * T + kt * 512 + 512]
                            b_lo = YLO[:, ch * T + kt * 512:
                                       ch * T + kt * 512 + 512]
                            for a, bb in ((a_hi, b_hi), (a_hi, b_lo),
                                          (a_lo, b_hi)):
                                ins.append(e.matmul(
                                    pst[:], a, bb,
                                    start=(ch == 0 and a is a_hi and bb is b_hi),
                                    stop=(ch == 7 and a is a_lo)))
                        return ins
                    sch.do("pe", mk,
                           reads=[wkb] + ybs[kt * 4:kt * 4 + 4],
                           writes=[pb])
                    split_hilo(KHI[:, kt * 512:(kt + 1) * 512],
                               KLO[:, kt * 512:(kt + 1) * 512], pst[:], pb, kb)
                load_w_cols(Wv, vcol, 64 if repl else 128, repl, WVH, WVL, wvb)
                for vt in range(T // 512):
                    pb, pst = psb[4 + (vt % 2)], PS[4 + (vt % 2)]
                    def mv(e, vt=vt, pst=pst):
                        ins = []
                        for ch in range(8):
                            a_hi = WVH[:, ch * 128:(ch + 1) * 128]
                            a_lo = WVL[:, ch * 128:(ch + 1) * 128]
                            b_hi = YHI[:, ch * T + vt * 512:
                                       ch * T + vt * 512 + 512]
                            b_lo = YLO[:, ch * T + vt * 512:
                                       ch * T + vt * 512 + 512]
                            for a, bb in ((a_hi, b_hi), (a_hi, b_lo),
                                          (a_lo, b_hi)):
                                ins.append(e.matmul(
                                    pst[:], a, bb,
                                    start=(ch == 0 and a is a_hi and bb is b_hi),
                                    stop=(ch == 7 and a is a_lo)))
                        return ins
                    sch.do("pe", mv,
                           reads=[wvb] + ybs[vt * 4:vt * 4 + 4],
                           writes=[pb])
                    sch.do("act", lambda e, vt=vt, pst=pst: e.activation(
                        VSTG[:, 0:512], pst[:],
                        AF.Copy), reads=[pb], writes=[vsb])
                    pb2, pst2 = psb[(vt + 1) % 2], PS[(vt + 1) % 2]
                    def tpv(e, vt=vt, pst2=pst2):
                        ins = []
                        for j in range(4):
                            ins.append(e.matmul(
                                pst2[:, j * 128:(j + 1) * 128],
                                VSTG[:, j * 128:(j + 1) * 128],
                                ID32[:], start=(j == 0), stop=(j == 3),
                                is_transpose=True))
                        return ins
                    sch.do("pe", tpv, reads=[vsb, idn], writes=[pb2])
                    va = VEXA[:, :].rearrange("p (c n) -> p c n", n=66)[
                        :, vt * 4:vt * 4 + 4, 0:64]
                    vbv = VEXB[:, :].rearrange("p (c n) -> p c n", n=66)[
                        :, vt * 4:vt * 4 + 4, 0:64]
                    pa = pst2[:, :].rearrange("p (c n) -> p c n", n=128)
                    sch.do("act", lambda e, va=va, pa=pa: e.activation(
                        va, pa[:, :, 0:64], AF.Copy), reads=[pb2], writes=[vb])
                    sch.do("act", lambda e, vbv=vbv, pa=pa: e.activation(
                        vbv, pa[:, :, 64:128], AF.Copy), reads=[pb2], writes=[vb])

            if tag == "A0_" and hp == 0:
                tap("khi", KHI[:, :], [128, T], BF16, [kb])
                tap("klo", KLO[:, :], [128, T], BF16, [kb])
                tap("vexa", VEXA[:, :], [128, 16 * 128], F32, [vb])
                tap("vexb", VEXB[:, :], [128, 16 * 128], F32, [vb])
            load_w_cols(Wq, hp * 128, 128, False, WQH, WQL, wqb)
            sch.do("sdma", lambda e, hp=hp: e.dma_start(
                out=WSTG[:, 0:1024], in_=Wo[hp * 128:(hp + 1) * 128, :]),
                writes=[wsb])
            split_hilo(WOH[:, 0:1024], WOL[:, 0:1024], WSTG[:, 0:1024],
                       wsb, wob)
            if kind == "a" and hp == 0:
                sch.do("sdma", lambda e: e.dma_start(
                    out=WSTG[0:1, 0:1024], in_=W["abo"].rearrange("(a d) -> a d", a=1)),
                    reads=[wsb], writes=[wsb])
                split_hilo(BORH[0:1, :], BORL[0:1, :], WSTG[0:1, 0:1024],
                           wsb, bob)

            for qt in range(nqt):
                q0 = qt * qw
                flush_tails()
                pbq, psq = psb[5], PS[5]
                def mq(e, q0=q0):
                    ins = []
                    for ch in range(8):
                        a_hi = WQH[:, ch * 128:(ch + 1) * 128]
                        a_lo = WQL[:, ch * 128:(ch + 1) * 128]
                        b_hi = YHI[:, ch * T + q0:ch * T + q0 + qw]
                        b_lo = YLO[:, ch * T + q0:ch * T + q0 + qw]
                        for a, bb in ((a_hi, b_hi), (a_hi, b_lo),
                                      (a_lo, b_hi)):
                            ins.append(e.matmul(
                                psq[:, 0:qw], a, bb,
                                start=(ch == 0 and a is a_hi and bb is b_hi),
                                stop=(ch == 7 and a is a_lo)))
                    return ins
                sch.do("pe", mq,
                       reads=[wqb] + ybs[(q0 // 128):(q0 + qw) // 128],
                       writes=[pbq])
                split_hilo(QHI[:, 0:qw], QLO[:, 0:qw], psq[:, 0:qw], pbq, qb)
                if tag == "A0_" and hp == 0 and qt == 0:
                    tap("qhi", QHI[:, 0:qw], [128, qw], BF16, [qb])
                    tap("qlo", QLO[:, 0:qw], [128, qw], BF16, [qb])

                if kind == "a":
                    chunks = [(None, kc) for kc in range(TT)]
                else:
                    i0 = q0
                    chunks = []
                    for j0 in range(max(0, i0 - WIN), i0 + qw, 128):
                        d0 = i0 - j0
                        if d0 == 256:
                            cls = 5 if j0 == 0 else 3
                        elif d0 == 128:
                            cls = 4 if j0 == 0 else 2
                        elif d0 == 0:
                            cls = 1
                        elif d0 == -128:
                            cls = 0
                        else:
                            cls = None
                        chunks.append((cls, j0 // 128))
                    if i0 - WIN > 0:
                        chunks.append(("sink", 0))

                oa_, psoa = psb[2], PS[2]
                ob2_, psob = psb[6], PS[6]
                ncht = len(chunks)
                pend_pv = []
                for ci, (cls, kc) in enumerate(chunks):
                    is_sink = (cls == "sink")
                    krows = SINK if is_sink else 128
                    for h in range(2):
                        sbk = 3 + ((2 * ci + h) % 3)
                        hb, psh = psb[sbk], PS[sbk]
                        r0 = h * 64
                        def msc(e, kc=kc, krows=krows, r0=r0, psh=psh):
                            ins = []
                            a_hi = KHI[r0:r0 + 64,
                                       kc * 128:kc * 128 + krows]
                            a_lo = KLO[r0:r0 + 64,
                                       kc * 128:kc * 128 + krows]
                            b_hi = QHI[r0:r0 + 64, 0:qw]
                            b_lo = QLO[r0:r0 + 64, 0:qw]
                            for i_, (a, bb) in enumerate(
                                    ((a_hi, b_hi), (a_hi, b_lo),
                                     (a_lo, b_hi))):
                                ins.append(e.matmul(
                                    psh[0:krows, 0:qw], a, bb,
                                    start=(i_ == 0), stop=(i_ == 2),
                                    tile_position=(r0, 0)))
                            return ins
                        sch.do("pe", msc, reads=[kb, qb], writes=[hb])
                        if kind == "b" and cls is not None and not is_sink:
                            sch.do("dve", lambda e, psh=psh, cls=cls,
                                   krows=krows: e.tensor_tensor(
                                       psh[0:krows, 0:qw], psh[0:krows, 0:qw],
                                       mask_t(cls)[0:krows, :], ALU.add),
                                   reads=[hb, mkb], writes=[hb])
                        ei = (2 * ci + h) % NE
                        et = ET[:, ei * ETW:ei * ETW + qw]
                        sch.do("act", lambda e, psh=psh, et=et, krows=krows:
                               e.activation(et[0:krows, :],
                                            psh[0:krows, 0:qw], AF.Exp,
                                            bias=0.0,
                                            scale=CB[0:krows,
                                                     S_SCALE:S_SCALE + 1]),
                               reads=[hb, cb], writes=[eb[ei]])
                        if tag == "A0_" and hp == 0 and qt == 0 and ci == 0:
                            tap(f"e{h}", et[0:krows, :], [krows, qw], F32,
                                [eb[ei]])
                        vsl = (VEXA if h == 0 else VEXB)[
                            :, kc * 66:kc * 66 + 65]
                        pv_first = (ci == 0)
                        pv_last = (ci == ncht - 1)

                        def dopv(vsl=vsl, et=et, krows=krows,
                                 pv_first=pv_first, pv_last=pv_last,
                                 pvb=(psoa if h == 0 else psob),
                                 ebx=eb[ei],
                                 obx=(oa_ if h == 0 else ob2_)):
                            def f(e):
                                ins = []
                                for qs_ in range(qw // 128):
                                    ins.append(e.matmul(
                                        pvb[:, qs_ * 66:qs_ * 66 + 65],
                                        et[0:krows,
                                           qs_ * 128:(qs_ + 1) * 128],
                                        vsl[0:krows, :],
                                        start=(pv_first and qs_ == 0),
                                        stop=(pv_last and
                                              qs_ == qw // 128 - 1)))
                                return ins
                            sch.do("pe", f, reads=[ebx, vb], writes=[obx])
                        pend_pv.append(dopv)
                        if len(pend_pv) > 2:
                            pend_pv.pop(0)()
                if True:
                    while pend_pv:
                        pend_pv.pop(0)()

                for qs_ in range(qw // 128):
                    for h in range(2):
                        ob_ = oa_ if h == 0 else ob2_
                        pso_ = psoa if h == 0 else psob
                        if kind == "b":
                            sch.do("dve", lambda e, qs_=qs_, h=h, pso_=pso_:
                                   e.tensor_single_scalar(
                                       RCQ[:, h:h + 1],
                                       pso_[:, qs_ * 66 + 64:qs_ * 66 + 65],
                                       TINY, ALU.max),
                                   reads=[ob_], writes=[rcq])
                            sch.do("dve", lambda e, h=h: e.reciprocal(
                                RCQ[:, h:h + 1], RCQ[:, h:h + 1]),
                                reads=[rcq], writes=[rcq])
                        else:
                            sch.do("dve", lambda e, qs_=qs_, h=h, pso_=pso_:
                                   e.reciprocal(
                                       RCQ[:, h:h + 1],
                                       pso_[:, qs_ * 66 + 64:qs_ * 66 + 65]),
                                   reads=[ob_], writes=[rcq])
                        sch.do("act", lambda e, qs_=qs_, h=h, pso_=pso_:
                               e.activation(
                                   ONT[:, qs_ * 128 + h * 64:
                                       qs_ * 128 + h * 64 + 64],
                                   pso_[:, qs_ * 66:qs_ * 66 + 64],
                                   AF.Copy, scale=RCQ[:, h:h + 1]),
                               reads=[ob_, rcq], writes=[onb])
                def dotail(qt=qt, q0=q0, hp=hp):
                    for qs_ in range(qw // 128):
                        pbT, psT = psb[qs_ % 2], PS[qs_ % 2]
                        sch.do("pe", lambda e, qs_=qs_, psT=psT: e.matmul(
                            psT[:, 0:128],
                            ONT[:, qs_ * 128:(qs_ + 1) * 128], ID32[:],
                            start=True, stop=True, is_transpose=True),
                            reads=[onb, idn], writes=[pbT])
                        split_hilo(ONHI[:, qs_ * 128:(qs_ + 1) * 128],
                                   ONLO[:, qs_ * 128:(qs_ + 1) * 128],
                                   psT[:, 0:128], pbT, onb)
                    for tg in range(qw // 128):
                        ti = (q0 + tg * 128) // 128
                        for half in range(2):
                            wb2, psw = psb[half], PS[half]
                            def mwo(e, tg=tg, half=half, hp=hp, psw=psw):
                                ins = []
                                a_hi = ONHI[:, tg * 128:(tg + 1) * 128]
                                a_lo = ONLO[:, tg * 128:(tg + 1) * 128]
                                b_hi = WOH[:, half * 512:(half + 1) * 512]
                                b_lo = WOL[:, half * 512:(half + 1) * 512]
                                for i_, (a, bb) in enumerate(
                                        ((a_hi, b_hi), (a_hi, b_lo),
                                         (a_lo, b_hi))):
                                    ins.append(e.matmul(
                                        psw[:], a, bb, start=(i_ == 0),
                                        stop=(i_ == 2 and not
                                              (kind == "a" and hp == 0))))
                                if kind == "a" and hp == 0:
                                    ins.append(e.matmul(
                                        psw[:], ONES1B[:, :],
                                        BORH[0:1, half * 512:(half + 1) * 512],
                                        start=False, stop=False))
                                    ins.append(e.matmul(
                                        psw[:], ONES1B[:, :],
                                        BORL[0:1, half * 512:(half + 1) * 512],
                                        start=False, stop=True))
                                return ins
                            sch.do("pe", mwo, reads=[onb, wob, bob, idn],
                                   writes=[wb2])
                            sch.do("dve", lambda e, ti=ti, half=half, psw=psw:
                                   e.tensor_tensor(
                                       XT[:, ti * D + half * 512:
                                          ti * D + half * 512 + 512],
                                       XT[:, ti * D + half * 512:
                                          ti * D + half * 512 + 512],
                                       psw[:], ALU.add),
                                   reads=[wb2, xtb[ti]], writes=[xtb[ti]])
                pend_tails.append(dotail)

        flush_tails()
        for i in range(TT):
            sch.do("gdma", lambda e, i=i, seq=seq: e.dma_start(
                out=dst_dram[seq, i * 128:(i + 1) * 128, :],
                in_=XT[:, i * D:(i + 1) * D]), reads=[xtb[i]])
        es.close()
        sch.set_fence()

    def ffn_phase(src_dram, dst_dram, first, tag):
        es = ExitStack()
        sb = lambda n, shp, dt: es.enter_context(nc.sbuf_tensor(tag + n, shp, dt))
        W1Q = sb("W1Q", [128, 8 * HF], FP8)
        w1b = Buf("W1Q")
        W2Q = sb("W2Q", [128, 32 * D], FP8)
        w2b = Buf("W2Q")
        H3 = sb("H3", [128, HF], F32)
        h3bs = [Buf(f"H3_{i}") for i in range(8)]
        H4 = H3
        h4bs = h3bs
        G1S = sb("G1S", [128, D], F32)
        g1b = Buf("G1S")
        H1S = sb("H1S", [128, D], F32)
        h1b = Buf("H1S")
        WBC = sb("WBC", [128, HF], F32)
        BBC = sb("BBC", [128, HF], F32)
        B1BC = sb("B1BC", [128, HF], F32)
        W1BC = sb("W1BC", [128, D], F32)
        B1LBC = sb("B1LBC", [128, D], F32)
        B2BC = sb("B2BC", [128, D], F32)
        bcb = Buf("BC")
        G3S = sb("G3S", [128, 1024], F32)
        g3bs = [Buf("G3S0"), Buf("G3S1")]
        US = sb("US", [128, 1024], F32)
        usbs = [Buf("US0"), Buf("US1")]
        UHS = sb("UHS", [128, 1024], F32)
        uhbs = [Buf("UHS0"), Buf("UHS1")]
        TQ = sb("TQ", [128, 1024], F32)
        tqbs = [Buf("TQ0"), Buf("TQ1")]
        K1N = sb("K1N", [128, D], BF16)
        K1F = sb("K1F", [128, D], BF16)
        k1b = Buf("K1")
        k1fb = Buf("K1F")
        K2N = sb("K2N", [128, HF], BF16)
        k2bs = [Buf(f"K2_{i}") for i in range(8)]
        K2F = sb("K2F", [128, 2048], BF16)
        k2fbs = [Buf("K2F0"), Buf("K2F1")]
        WSTG = sb("WSTGF", [128, 1024], F32)
        wsb = Buf("WSF")
        XIN = sb("XIN", [128, 2 * D], F32)
        xib = [Buf("XI0"), Buf("XI1")]
        XOUT = sb("XOUT", [128, D], F32)
        xob = Buf("XO")
        YT = sb("YTF", [128, D], F32)
        ytb = Buf("YTF")
        sch.set_fence()

        def quant_tile(src_ap, src_bufs, ncols, dst_t, dst_bufs, s_ap, s_buf):
            nsl = ncols // 512
            for s in range(nsl):
                sl = slice(s * 512, (s + 1) * 512)
                sbuf = src_bufs[min(s, len(src_bufs) - 1)]
                dbuf = dst_bufs[min(s, len(dst_bufs) - 1)]
                us = US[:, (s % 2) * 512:(s % 2) * 512 + 512]
                g3 = G3S[:, (s % 2) * 512:(s % 2) * 512 + 512]
                ts_ = TQ[:, (s % 2) * 512:(s % 2) * 512 + 512]
                sch.do("act", lambda e, sl=sl, us=us: e.activation(
                    us, src_ap[:, sl], AF.Identity, bias=c(S_C1), scale=s_ap),
                    reads=[sbuf, s_buf, cb], writes=[usbs[s % 2]])
                sch.do("act", lambda e, sl=sl, g3=g3: e.activation(
                    g3, src_ap[:, sl], AF.Identity, bias=c(S_C3), scale=s_ap),
                    reads=[sbuf, s_buf, cb], writes=[g3bs[s % 2]])
                sch.do("dve", lambda e, us=us, ts_=ts_: e.tensor_scalar(
                    ts_, us, c(S_APN), c(S_KNN), ALU.mult, ALU.add),
                    reads=[usbs[s % 2], cb], writes=[tqbs[s % 2]])
                sch.do("dve", lambda e, sl=sl, g3=g3, ts_=ts_:
                       e.scalar_tensor_tensor(
                           dst_t[:, sl], g3, c(S_BPN), ts_,
                           ALU.mult, ALU.add),
                       reads=[g3bs[s % 2], tqbs[s % 2], cb], writes=[dbuf])

        def quantize_weights():
            for wd, nrow, wcols, dst_t, dbuf, spill, qcol, tile_cols in (
                    (W["fw1"], HF, D, W1Q, w1b, w1q_sp, 0, D),
                    (W["fw2"], D, HF, W2Q, w2b, w2q_sp, 2, 1024)):
                ntr = nrow // 128
                nct = wcols // tile_cols
                first_acc = [True]
                for it in range(ntr):
                    for jt in range(nct):
                        sch.do("sdma", lambda e, it=it, jt=jt, wd=wd, tile_cols=tile_cols:
                               e.dma_start(
                            out=WSTG[:, 0:tile_cols],
                            in_=wd[it * 128:(it + 1) * 128,
                                   jt * tile_cols:(jt + 1) * tile_cols]),
                            writes=[wsb])
                        sch.do("dve", lambda e, tile_cols=tile_cols: e.reduce_sum(
                            STATS[:, 6:7], WSTG[:, 0:tile_cols], axis=AX.X,
                            apply_absolute_value=True),
                            reads=[wsb], writes=[stbs[0]])
                        fa = first_acc[0]
                        first_acc[0] = False
                        if fa:
                            sch.do("dve", lambda e: e.tensor_copy(
                                STATS[:, 5:6], STATS[:, 6:7]),
                                reads=[stbs[0]], writes=[stbs[0]])
                        else:
                            sch.do("dve", lambda e: e.tensor_tensor(
                                STATS[:, 5:6], STATS[:, 5:6], STATS[:, 6:7],
                                ALU.add), reads=[stbs[0]], writes=[stbs[0]])
                pb, pst = psb[0], PS[0]
                sch.do("pe", lambda e: e.matmul(
                    pst[0:1, 0:1], ONES1[:, 0:1], STATS[:, 5:6],
                    start=True, stop=True), reads=[stbs[0], idn], writes=[pb])
                sch.do("dve", lambda e: e.tensor_copy(
                    STATS2[0:1, 8:9], pst[0:1, 0:1]), reads=[pb],
                    writes=[stb2s[8]])
                sch.do("pe", lambda e: e.matmul(
                    pst[:, 0:1], ONESR[:, :], STATS2[0:1, 8:9],
                    start=True, stop=True), reads=[stb2s[8], idn], writes=[pb])
                mi = 1.0 / float(nrow * wcols)
                sch.do("dve", lambda e, mi=mi, qcol=qcol: e.tensor_scalar(
                    QS[:, qcol:qcol + 1], pst[:, 0:1], mi, 0.0,
                    ALU.mult, ALU.add), reads=[pb], writes=[qs])
                sch.do("dve", lambda e, qcol=qcol: e.tensor_tensor(
                    QS[:, qcol:qcol + 1], QS[:, qcol:qcol + 1], c(S_EPS),
                    ALU.max), reads=[qs, cb], writes=[qs])
                sch.do("dve", lambda e, qcol=qcol: e.reciprocal(
                    QS[:, qcol + 1:qcol + 2], QS[:, qcol:qcol + 1]),
                    reads=[qs], writes=[qs])
                for it in range(ntr):
                    for jt in range(nct):
                        sch.do("sdma", lambda e, it=it, jt=jt, wd=wd, tile_cols=tile_cols:
                               e.dma_start(
                            out=WSTG[:, 0:tile_cols],
                            in_=wd[it * 128:(it + 1) * 128,
                                   jt * tile_cols:(jt + 1) * tile_cols]),
                            writes=[wsb])
                        quant_tile(WSTG, [wsb], tile_cols, K2N, [k2bs[0]],
                                   QS[:, qcol + 1:qcol + 2], qs)
                        sch.do("dve", lambda e, tile_cols=tile_cols: e.tensor_scalar(
                            K2N[:, 0:tile_cols], K2N[:, 0:tile_cols],
                            1.0, -1.0, ALU.min, ALU.max),
                            reads=[k2bs[0]], writes=[k2bs[0]])
                        ncc = tile_cols // 128
                        for c8 in range(0, ncc, 8):
                            n8 = min(8, ncc - c8)
                            def tpq(e, c8=c8, n8=n8):
                                ins = []
                                for j in range(n8):
                                    ins.append(e.matmul(
                                        PT[:, j * 128:(j + 1) * 128],
                                        K2N[:, (c8 + j) * 128:
                                            (c8 + j) * 128 + 128],
                                        IDBF[:], start=(j == 0),
                                        stop=(j == n8 - 1),
                                        is_transpose=True))
                                return ins
                            sch.do("pe", tpq, reads=[k2bs[0], idn], writes=[ptb])
                            def ev(e, it=it, jt=jt, c8=c8, n8=n8,
                                   dst_t=dst_t, ncc=ncc, nrow=nrow):
                                dv = dst_t[:, :].rearrange(
                                    "p (c n) -> p c n", n=nrow)
                                ch0 = jt * ncc + c8
                                return e.tensor_copy(
                                    dv[:, ch0:ch0 + n8,
                                       it * 128:it * 128 + 128],
                                    PT[:, 0:n8 * 128].rearrange(
                                        "p (c n) -> p c n", c=n8))
                            sch.do("dve", ev, reads=[ptb], writes=[dbuf])
                sch.do("gdma", lambda e, dst_t=dst_t, spill=spill:
                       e.dma_start(out=spill, in_=dst_t[:, :]),
                       reads=[dbuf])

        if first:
            quantize_weights()
        else:
            for dst_t, dbuf, spill in ((W1Q, w1b, w1q_sp),
                                       (W2Q, w2b, w2q_sp)):
                sch.do("gdma", lambda e, dst_t=dst_t, spill=spill:
                       e.dma_start(out=dst_t[:, :], in_=spill),
                       writes=[dbuf])

        def bcast(dst_t, row_dram, ncols):
            sch.do("sdma", lambda e: e.dma_start(
                out=H3[0:1, 0:ncols], in_=row_dram.rearrange("(a d) -> a d", a=1)),
                writes=list(h3bs))
            for s0 in range(0, ncols, 512):
                pb, pst = psb[0], PS[0]
                sch.do("pe", lambda e, s0=s0: e.matmul(
                    pst[:], ONESR[:, :], H3[0:1, s0:s0 + 512],
                    start=True, stop=True), reads=list(h3bs), writes=[pb])
                sch.do("act", lambda e, s0=s0: e.activation(
                    dst_t[:, s0:s0 + 512], pst[:], AF.Copy),
                    reads=[pb], writes=[bcb])

        bcast(WBC, W["fln2w"], HF)
        bcast(BBC, W["fln2b"], HF)
        bcast(B1BC, W["fb1"], HF)
        bcast(W1BC, W["fln1w"], D)
        bcast(B1LBC, W["fln1b"], D)
        bcast(B2BC, W["fb2"], D)

        def amax_partial(src_ap, src_buf, s, par):
            m0 = 32 + 8 * par
            sch.do("dve", lambda e, s=s, m0=m0: e.reduce_max(
                STATS2[:, m0 + s:m0 + s + 1],
                src_ap[:, s * 512:(s + 1) * 512], axis=AX.X,
                apply_absolute_value=True),
                reads=[src_buf], writes=[stb2s[m0 + s]])

        def amax_scale(nsl, ds_slot, mw_col, par):
            b0 = 16 * par
            m0 = 32 + 8 * par
            stb = stbs[par]

            def S(i):
                return STATS[:, b0 + i:b0 + i + 1]

            sch.do("dve", lambda e, nsl=nsl, m0=m0: e.reduce_max(
                S(2), STATS2[:, m0:m0 + nsl], axis=AX.X),
                reads=[stb2s[m0 + s] for s in range(nsl)], writes=[stb])
            sch.do("dve", lambda e: e.tensor_tensor(
                S(2), S(2), c(S_EPS), ALU.max), reads=[stb, cb], writes=[stb])
            sch.do("dve", lambda e: e.reciprocal(S(3), S(2)),
                   reads=[stb], writes=[stb])
            sch.do("dve", lambda e: e.tensor_single_scalar(
                S(3), S(3), 127.0, ALU.mult), reads=[stb], writes=[stb])
            sch.do("dve", lambda e, ds_slot=ds_slot, mw_col=mw_col:
                   e.tensor_tensor(S(ds_slot), S(2),
                                   QS[:, mw_col:mw_col + 1], ALU.mult),
                   reads=[stb, qs], writes=[stb])
            sch.do("dve", lambda e, ds_slot=ds_slot: e.tensor_single_scalar(
                S(ds_slot), S(ds_slot), 1.0 / 127.0, ALU.mult),
                reads=[stb], writes=[stb])

        def prelude(gidx):
            seq2, it = divmod(gidx, TT)
            par = gidx % 2

            def mkS(i, par=par):
                return STATS[:, 16 * par + i:16 * par + i + 1]

            stb = stbs[par]
            xsel = gidx % 2
            xi = XIN[:, xsel * D:(xsel + 1) * D]
            xbuf = xib[xsel]
            sch.do("gdma", lambda e, it=it, xi=xi, seq2=seq2:
                   e.dma_start(out=xi, in_=src_dram[
                       seq2, it * 128:(it + 1) * 128, :]), writes=[xbuf])
            ln_stats(xi, xbuf, D, par)
            sch.do("dve", lambda e, xi=xi, s0=mkS(0), s3=mkS(3):
                   e.tensor_scalar(YT[:], xi, s0, s3,
                                   ALU.subtract, ALU.mult),
                   reads=[xbuf, stb], writes=[ytb])
            ln_stats(YT[:], ytb, D, par)
            sch.do("dve", lambda e, s6=mkS(6), s0=mkS(0), s3=mkS(3):
                   e.tensor_tensor(s6, s0, s3, ALU.mult),
                   reads=[stb], writes=[stb])
            sch.do("dve", lambda e, s6=mkS(6): e.tensor_single_scalar(
                s6, s6, -1.0, ALU.mult), reads=[stb], writes=[stb])
            sch.do("act", lambda e, s6=mkS(6), s3=mkS(3): e.activation(
                G1S[:], YT[:], AF.Identity, bias=s6, scale=s3),
                reads=[ytb, stb], writes=[g1b])
            sch.do("dve", lambda e: e.tensor_tensor(
                H1S[:], G1S[:], W1BC[:], ALU.mult),
                reads=[g1b, bcb], writes=[h1b])
            sch.do("dve", lambda e: e.tensor_tensor(
                H1S[:], H1S[:], B1LBC[:], ALU.add),
                reads=[h1b, bcb], writes=[h1b])
            for s in range(2):
                amax_partial(H1S, h1b, s, par)
            amax_scale(2, 9, 0, par)
            quant_tile(H1S, [h1b], D, K1N, [k1b], mkS(3), stb)
            def tk1(e):
                ins = []
                for j in range(8):
                    ins.append(e.matmul(
                        PT[:, j * 128:(j + 1) * 128],
                        K1N[:, j * 128:(j + 1) * 128], IDBF[:],
                        start=(j == 0), stop=(j == 7), is_transpose=True))
                return ins
            sch.do("pe", tk1, reads=[k1b, idn], writes=[ptb])
            sch.do("act", lambda e: e.activation(K1F[:], PT[:, 0:D],
                                                 AF.Copy),
                   reads=[ptb], writes=[k1fb])
            return xi, xbuf, par

        def body(gidx):
            par = gidx % 2
            a0 = 16 * par
            stb = stbs[par]

            def mkS(i, par=par):
                return STATS[:, 16 * par + i:16 * par + i + 1]

            sch.do("dve", lambda e, a0=a0: e.memset(
                STATS2[:, a0:a0 + 16], 0.0),
                writes=[stb2s[a0 + i] for i in range(16)])
            for hs in range(HF // 512):
                pb, pst = psb[hs % 2], PS[hs % 2]
                def m1(e, hs=hs, pst=pst):
                    ins = []
                    for chn in range(8):
                        ins.append(e.matmul(
                            pst[:], K1F[:, chn * 128:(chn + 1) * 128],
                            W1Q[:, chn * HF + hs * 512:
                                     chn * HF + hs * 512 + 512],
                            start=(chn == 0), stop=(chn == 7)))
                    return ins
                sch.do("pe", m1, reads=[k1fb, w1b], writes=[pb])
                uh = UHS[:, (hs % 2) * 512:(hs % 2) * 512 + 512]
                sch.do("dve", lambda e, hs=hs, pst=pst, uh=uh,
                       s9=mkS(9): e.scalar_tensor_tensor(
                           uh, pst[:], s9,
                           B1BC[:, hs * 512:hs * 512 + 512],
                           ALU.mult, ALU.add),
                       reads=[pb, stb, bcb], writes=[uhbs[hs % 2]])
                sch.do("act", lambda e, hs=hs, uh=uh, a0=a0: e.activation(
                    H3[:, hs * 512:hs * 512 + 512], uh, AF.Gelu,
                    accum_out=STATS2[:, a0 + hs:a0 + hs + 1]),
                    reads=[uhbs[hs % 2]],
                    writes=[h3bs[hs], stb2s[a0 + hs]])
                sch.do("act", lambda e, hs=hs, a0=a0: e.activation(
                    SQSCR[:, (hs % 2) * 512:(hs % 2) * 512 + 512],
                    H3[:, hs * 512:(hs + 1) * 512],
                    AF.Square, accum_out=STATS2[:, a0 + 8 + hs:
                                                a0 + 9 + hs]),
                    reads=[h3bs[hs]],
                    writes=[sqbs[hs % 2], stb2s[a0 + 8 + hs]])

        def tail(gidx, xi, xbuf):
            seq2, it = divmod(gidx, TT)
            par = gidx % 2
            a0 = 16 * par
            stb = stbs[par]

            def mkS(i, par=par):
                return STATS[:, 16 * par + i:16 * par + i + 1]

            def warm(rd):
                sch.do("pe", lambda e: e.matmul(
                    PS[6][0:1, 0:128], IDBF[:, 0:1], IDBF[:, 0:128],
                    start=True, stop=True), reads=rd + [idn],
                    writes=[psb[6]])

            sch.do("dve", lambda e, a0=a0, s0=mkS(0): e.reduce_sum(
                s0, STATS2[:, a0:a0 + 8], axis=AX.X),
                reads=[stb2s[a0 + i] for i in range(8)], writes=[stb])
            sch.do("dve", lambda e, a0=a0, s1=mkS(1): e.reduce_sum(
                s1, STATS2[:, a0 + 8:a0 + 16], axis=AX.X),
                reads=[stb2s[a0 + 8 + i] for i in range(8)], writes=[stb])
            inv = 1.0 / HF
            sch.do("dve", lambda e, s0=mkS(0): e.tensor_single_scalar(
                s0, s0, inv, ALU.mult), reads=[stb], writes=[stb])
            sch.do("dve", lambda e, s2=mkS(2), s0=mkS(0): e.tensor_tensor(
                s2, s0, s0, ALU.mult), reads=[stb], writes=[stb])
            sch.do("dve", lambda e, s1=mkS(1): e.tensor_scalar(
                s1, s1, inv, 0.0, ALU.mult, ALU.add),
                reads=[stb], writes=[stb])
            sch.do("dve", lambda e, s1=mkS(1), s2=mkS(2): e.tensor_tensor(
                s1, s1, s2, ALU.subtract), reads=[stb], writes=[stb])
            sch.do("act", lambda e, s6=mkS(6), s1=mkS(1): e.activation(
                s6, s1, AF.Sqrt, bias=c(S_EPS), scale=1.0),
                reads=[stb, cb], writes=[stb])
            sch.do("dve", lambda e, s3=mkS(3), s6=mkS(6): e.reciprocal(
                s3, s6), reads=[stb], writes=[stb])
            sch.do("dve", lambda e, s6=mkS(6), s3=mkS(3): e.tensor_tensor(
                s6, s3, s3, ALU.mult), reads=[stb], writes=[stb])
            sch.do("dve", lambda e, s7=mkS(7), s1=mkS(1):
                   e.tensor_single_scalar(s7, s1, EPS, ALU.add),
                   reads=[stb], writes=[stb])
            sch.do("dve", lambda e, s6=mkS(6), s7=mkS(7): e.tensor_tensor(
                s6, s6, s7, ALU.mult), reads=[stb], writes=[stb])
            sch.do("dve", lambda e, s6=mkS(6): e.tensor_scalar(
                s6, s6, -0.5, 1.5, ALU.mult, ALU.add),
                reads=[stb], writes=[stb])
            sch.do("dve", lambda e, s3=mkS(3), s6=mkS(6): e.tensor_tensor(
                s3, s3, s6, ALU.mult), reads=[stb], writes=[stb])
            sch.do("dve", lambda e, s6=mkS(6), s0=mkS(0), s3=mkS(3):
                   e.tensor_tensor(s6, s0, s3, ALU.mult),
                   reads=[stb], writes=[stb])
            sch.do("dve", lambda e, s6=mkS(6): e.tensor_single_scalar(
                s6, s6, -1.0, ALU.mult), reads=[stb], writes=[stb])
            warm([stb])
            for hs in range(HF // 512):
                sl = slice(hs * 512, hs * 512 + 512)
                g3 = G3S[:, (hs % 2) * 512:(hs % 2) * 512 + 512]
                sch.do("act", lambda e, sl=sl, g3=g3, s6=mkS(6),
                       s3=mkS(3): e.activation(
                    g3, H3[:, sl], AF.Identity, bias=s6, scale=s3),
                    reads=[h3bs[hs], stb], writes=[g3bs[hs % 2]])
                sch.do("dve", lambda e, sl=sl, g3=g3: e.tensor_tensor(
                    H4[:, sl], g3, WBC[:, sl], ALU.mult),
                    reads=[g3bs[hs % 2], bcb], writes=[h4bs[hs]])
                sch.do("gp", lambda e, sl=sl: e.tensor_tensor(
                    H4[:, sl], H4[:, sl], BBC[:, sl], ALU.add),
                    reads=[h4bs[hs], bcb], writes=[h4bs[hs]])
                amax_partial(H4, h4bs[hs], hs, par)
                warm([h4bs[hs]])
            amax_scale(8, 10, 2, par)
            quant_tile(H4, h4bs, HF, K2N, k2bs, mkS(3), stb)
            warm([k2bs[0]])
            warm([k2bs[1]])
            pend_m2 = []
            for g8 in range(4):
                kpar = g8 % 2
                def tk2(e, g8=g8):
                    ins = []
                    for j in range(8):
                        ins.append(e.matmul(
                            PT[:, j * 128:(j + 1) * 128],
                            K2N[:, (g8 * 8 + j) * 128:
                                (g8 * 8 + j) * 128 + 128], IDBF[:],
                            start=(j == 0), stop=(j == 7),
                            is_transpose=True))
                    return ins
                sch.do("pe", tk2, reads=[k2bs[g8 * 2], k2bs[g8 * 2 + 1],
                                         idn], writes=[ptb])
                sch.do("act", lambda e, kpar=kpar: e.activation(
                    K2F[:, kpar * 1024:(kpar + 1) * 1024], PT[:, 0:1024],
                    AF.Copy), reads=[ptb], writes=[k2fbs[kpar]])
                for ds_ in range(2):
                    pb, pst = psb[2 + ds_], PS[2 + ds_]
                    def m2(e, g8=g8, ds_=ds_, pst=pst, kpar=kpar):
                        ins = []
                        for j in range(8):
                            chn = g8 * 8 + j
                            ins.append(e.matmul(
                                pst[:],
                                K2F[:, kpar * 1024 + j * 128:
                                    kpar * 1024 + (j + 1) * 128],
                                W2Q[:, chn * D + ds_ * 512:
                                         chn * D + ds_ * 512 + 512],
                                start=(chn == 0), stop=(chn == 31)))
                        return ins
                    pend_m2.append((m2, k2fbs[kpar], psb[2 + ds_]))
                if g8 >= 1:
                    f, rb, wb = pend_m2.pop(0)
                    sch.do("pe", f, reads=[rb, w2b], writes=[wb])
                    f, rb, wb = pend_m2.pop(0)
                    sch.do("pe", f, reads=[rb, w2b], writes=[wb])
            for f, rb, wb in pend_m2:
                sch.do("pe", f, reads=[rb, w2b], writes=[wb])
            for ds_ in range(2):
                pb, pst = psb[2 + ds_], PS[2 + ds_]
                sch.do("dve", lambda e, ds_=ds_, pst=pst, s10=mkS(10):
                       e.scalar_tensor_tensor(
                           XOUT[:, ds_ * 512:ds_ * 512 + 512], pst[:],
                           s10, B2BC[:, ds_ * 512:ds_ * 512 + 512],
                           ALU.mult, ALU.add),
                       reads=[pb, stb, bcb], writes=[xob])
                sch.do("dve", lambda e, ds_=ds_, xi=xi: e.tensor_tensor(
                    XOUT[:, ds_ * 512:ds_ * 512 + 512],
                    XOUT[:, ds_ * 512:ds_ * 512 + 512],
                    xi[:, ds_ * 512:ds_ * 512 + 512], ALU.add),
                    reads=[xob, xbuf], writes=[xob])
            sch.do("gdma", lambda e, it=it, seq2=seq2: e.dma_start(
                out=dst_dram[seq2, it * 128:(it + 1) * 128, :],
                in_=XOUT[:]), reads=[xob])

        NTILE = SEQ * TT
        info = prelude(0)
        for g in range(NTILE):
            body(g)
            cur = info
            if g + 1 < NTILE:
                info = prelude(g + 1)
            tail(g, cur[0], cur[1])
        es.close()
        sch.set_fence()

    if trunc == "phaseA":
        attention_phase("a", x_in, xs_a, 0, "A0_")
        sch.do("dve", lambda e: e.engine_nop(), extra=sch.fence)
        sch.emit()
        return nc
    for seq in range(SEQ):
        attention_phase("a", x_in, xs_a, seq, f"A{seq}_")
    ffn_phase(xs_a, xs_b, True, "F0_")
    for seq in range(SEQ):
        attention_phase("b", xs_b, xs_c, seq, f"B{seq}_")
    ffn_phase(xs_c, out_d, False, "F1_")
    sch.do("dve", lambda e: e.engine_nop(), extra=sch.fence)
    sch.emit()
    return nc


_NC_CACHE = [None]


def kernel(**inputs):
    x = np.ascontiguousarray(inputs["L0_attn_a_x"], dtype=np.float32)
    shards = np.split(x, NCORES, axis=0)
    wmap = {
        "aWq": "L0_attn_a_Wq", "aWk": "L0_attn_a_Wk", "aWv": "L0_attn_a_Wv",
        "aWo": "L0_attn_a_Wo", "abo": "L0_attn_a_bo",
        "bWq": "L0_attn_b_Wq", "bWk": "L0_attn_b_Wk", "bWv": "L0_attn_b_Wv",
        "bWo": "L0_attn_b_Wo",
        "fw1": "L0_ffn_w1", "fb1": "L0_ffn_b1",
        "fln1w": "L0_ffn_ln1_w", "fln1b": "L0_ffn_ln1_b",
        "fln2w": "L0_ffn_ln2_w", "fln2b": "L0_ffn_ln2_b",
        "fw2": "L0_ffn_w2", "fb2": "L0_ffn_b2",
    }
    if _NC_CACHE[0] is None:
        _NC_CACHE[0] = build_program()
    nc = _NC_CACHE[0]
    in_maps = []
    for s in shards:
        m = {"x": np.ascontiguousarray(s)}
        for k, v in wmap.items():
            m[k] = np.ascontiguousarray(inputs[v], dtype=np.float32)
        in_maps.append(m)
    res = run_bass_kernel_spmd(nc, in_maps, core_ids=list(range(NCORES)))
    return np.concatenate([r["out"] for r in res.results], axis=0)


# version 34
# speedup vs baseline: 1.0415x; 1.0415x over previous
import numpy as np
from contextlib import ExitStack
import concourse.bass as bass
import concourse.mybir as mybir
from concourse.bass_utils import run_bass_kernel_spmd

F32 = mybir.dt.float32
BF16 = mybir.dt.bfloat16
FP8 = mybir.dt.float8e4
AF = mybir.ActivationFunctionType
ALU = mybir.AluOpType
AX = mybir.AxisListType

B = 16
NCORES = 8
SEQ = B // NCORES
T = 2048
D = 1024
HD = 64
NH = 16
KVH = 4
HF = 4096
SINK = 4
WIN = 256
EPS = 1e-5
C1 = float(3 * 2 ** 22)
C3 = float(3 * 2 ** 51)
NEG = float(np.finfo(np.float32).min)
TINY = float(np.finfo(np.float32).tiny)
TT = T // 128
QW = 512
QWB = 256


class _Step:
    __slots__ = ("eng", "cum")

    def __init__(self, eng, cum):
        self.eng, self.cum = eng, cum


class Buf:
    __slots__ = ("w", "r", "name")

    def __init__(self, name=""):
        self.w, self.r, self.name = None, [], name


class Sched:
    DMA = ("sdma", "gdma", "adma")

    def __init__(self, nc):
        self.nc = nc
        self.order = []
        self.cum = {e: 0 for e in ("pe", "act", "dve", "gp",
                                   "sdma", "gdma", "adma")}
        self.fence = []
        self.last_dma = {"sdma": None, "gdma": None, "adma": None}

    def do(self, eng, fn, reads=(), writes=(), extra=()):
        deps = list(self.fence)
        for b in reads:
            if b.w is not None:
                deps.append(b.w)
        for b in writes:
            if b.w is not None:
                deps.append(b.w)
            deps.extend(b.r)
        deps.extend(x for x in extra if x is not None)
        if eng in self.DMA and self.last_dma[eng] is not None:
            deps.append(self.last_dma[eng])
        inc = 16 if eng in self.DMA else 1
        self.cum[eng] += inc
        st = _Step(eng, self.cum[eng])
        self.order.append((st, fn, deps))
        for b in reads:
            b.r.append(st)
        for b in writes:
            b.w = st
            b.r = []
        if eng in self.DMA:
            self.last_dma[eng] = st
        return st

    def set_fence(self):
        last = {}
        for st, _, _ in self.order:
            last[st.eng] = st
        self.fence = list(last.values())

    def emit(self):
        nc = self.nc
        plans = {e: [] for e in ("pe", "act", "dve", "gpsimd", "sdma")}
        for st, fn, deps in self.order:
            need = {}
            for d in deps:
                if d.eng == st.eng == "pe":
                    continue
                if d.cum > need.get(d.eng, 0):
                    need[d.eng] = d.cum
            stream = {"gp": "gpsimd", "gdma": "gpsimd",
                      "adma": "act"}.get(st.eng, st.eng)
            plans[stream].append((st, fn, need))

        with ExitStack() as es:
            sems = {e: es.enter_context(nc.semaphore(f"sem_{e}"))
                    for e in ("pe", "act", "dve", "gp",
                              "sdma", "gdma", "adma")}
            block = es.enter_context(nc.Block())
            waited = {s: {e: 0 for e in sems} for s in plans}

            def run(eng_obj, stream):
                for st, fn, need in plans[stream]:
                    for e in ("pe", "act", "dve", "gp",
                              "sdma", "gdma", "adma"):
                        v = need.get(e, 0)
                        if waited[stream][e] < v:
                            eng_obj.wait_ge(sems[e], v)
                            waited[stream][e] = v
                    inst = fn(eng_obj)
                    if isinstance(inst, (list, tuple)):
                        inst = inst[-1]
                    inst.then_inc(sems[st.eng], 16 if st.eng in self.DMA else 1)

            if plans["sdma"]:
                @block.sync
                def _(sync):
                    run(sync, "sdma")
            if plans["gpsimd"]:
                @block.gpsimd
                def _(g):
                    run(g, "gpsimd")
            if plans["pe"]:
                @block.tensor
                def _(t):
                    run(t, "pe")
            if plans["act"]:
                @block.scalar
                def _(sc):
                    run(sc, "act")
            if plans["dve"]:
                @block.vector
                def _(v):
                    run(v, "dve")


def build_program(debug=False, trunc=None):
    nc = bass.Bass("TRN2", target_bir_lowering=False)
    sch = Sched(nc)

    x_in = nc.dram_tensor("x", [SEQ, T, D], F32, kind="ExternalInput").ap()
    out_d = nc.dram_tensor("out", [SEQ, T, D], F32, kind="ExternalOutput").ap()
    W = {}
    for nm, shp in [
        ("aWq", [D, D]), ("aWk", [D, D]), ("aWv", [D, D]), ("aWo", [D, D]),
        ("abo", [D]), ("bWq", [D, D]), ("bWk", [D, KVH * HD]),
        ("bWv", [D, KVH * HD]), ("bWo", [D, D]), ("fw1", [HF, D]),
        ("fb1", [HF]), ("fln1w", [D]), ("fln1b", [D]), ("fln2w", [HF]),
        ("fln2b", [HF]), ("fw2", [D, HF]), ("fb2", [D]),
    ]:
        W[nm] = nc.dram_tensor(nm, shp, F32, kind="ExternalInput").ap()
    kd = "ExternalOutput" if debug else "Internal"
    xs_a = nc.dram_tensor("xs_a", [SEQ, T, D], F32, kind=kd).ap()
    xs_b = nc.dram_tensor("xs_b", [SEQ, T, D], F32, kind=kd).ap()
    xs_c = nc.dram_tensor("xs_c", [SEQ, T, D], F32, kind=kd).ap()
    w1q_sp = nc.dram_tensor("w1q_sp", [128, 8 * HF], FP8, kind="Internal").ap()
    w2q_sp = nc.dram_tensor("w2q_sp", [128, 32 * D], FP8, kind="Internal").ap()

    CB = nc.alloc_sbuf_tensor("CB", [128, 16], F32)
    cb = Buf("CB")
    STATS = nc.alloc_sbuf_tensor("STATS", [128, 32], F32)
    stbs = [Buf("STATS0"), Buf("STATS1")]
    STATS2 = nc.alloc_sbuf_tensor("STATS2", [128, 48], F32)
    stb2s = [Buf(f"S2_{i}") for i in range(48)]
    SQSCR = nc.alloc_sbuf_tensor("SQSCR", [128, 1024], F32)
    sqbs = [Buf("SQ0"), Buf("SQ1")]
    ID32 = nc.alloc_sbuf_tensor("ID32", [128, 128], F32)
    IDBF = nc.alloc_sbuf_tensor("IDBF", [128, 128], BF16)
    ONES1 = nc.alloc_sbuf_tensor("ONES1", [128, 1], F32)
    ONESR = nc.alloc_sbuf_tensor("ONESR", [1, 128], F32)
    ONES1B = nc.alloc_sbuf_tensor("ONES1B", [1, 128], BF16)
    SELA = nc.alloc_sbuf_tensor("SELA", [1, 128], F32)
    SELB = nc.alloc_sbuf_tensor("SELB", [1, 128], F32)
    idn = Buf("ID")
    QS = nc.alloc_sbuf_tensor("QS", [128, 4], F32)
    qs = Buf("QS")

    (S_DISC, S_B1, S_APN, S_BPN, S_KNN, S_C1, S_EPS, S_SCALE, S_TINY,
     S_T0, S_T1, S_T2, S_C3) = range(13)

    def c(i):
        return CB[:, i:i + 1]

    def setup_consts():
        for slot, val in ((S_T0, float(2 ** 24)), (S_EPS, EPS),
                          (S_SCALE, 0.125), (S_TINY, TINY), (S_C1, C1),
                          (S_C3, C3)):
            sch.do("gp", lambda e, s=slot, v=val: e.memset(c(s), v), writes=[cb])
        sch.do("dve", lambda e: e.tensor_single_scalar(c(S_T1), c(S_T0), 1.0,
                                                       ALU.add),
               reads=[cb], writes=[cb])
        sch.do("dve", lambda e: e.tensor_tensor(c(S_DISC), c(S_T1), c(S_T0),
                                                ALU.is_equal),
               reads=[cb], writes=[cb])
        sch.do("dve", lambda e: e.tensor_scalar(c(S_BPN), c(S_DISC), 1.0, -1.0,
                                                ALU.mult, ALU.add),
               reads=[cb], writes=[cb])
        sch.do("dve", lambda e: e.tensor_single_scalar(c(S_APN), c(S_DISC),
                                                       -1.0, ALU.mult),
               reads=[cb], writes=[cb])
        sch.do("dve", lambda e: e.tensor_single_scalar(c(S_T0), c(S_DISC), C1,
                                                       ALU.mult),
               reads=[cb], writes=[cb])
        sch.do("dve", lambda e: e.tensor_scalar(c(S_T1), c(S_BPN), -C3, 0.0,
                                                ALU.mult, ALU.add),
               reads=[cb], writes=[cb])
        sch.do("dve", lambda e: e.tensor_tensor(c(S_KNN), c(S_T0), c(S_T1),
                                                ALU.add),
               reads=[cb], writes=[cb])
        sch.do("gp", lambda e: e.memset(ID32[:], 1.0), writes=[idn])
        sch.do("gp", lambda e: e.affine_select(
            ID32[:], ID32[:], [[1, 128]], ALU.is_equal, 0.0,
            base=0, channel_multiplier=-1), reads=[idn], writes=[idn])
        sch.do("dve", lambda e: e.tensor_copy(IDBF[:], ID32[:]),
               reads=[idn], writes=[idn])
        sch.do("gp", lambda e: e.memset(ONES1[:], 1.0), writes=[idn])
        sch.do("gp", lambda e: e.memset(ONESR[:], 1.0), writes=[idn])
        sch.do("gp", lambda e: e.memset(ONES1B[:], 1.0), writes=[idn])
        sch.do("gp", lambda e: e.memset(SELA[:], 1.0), writes=[idn])
        sch.do("gp", lambda e: e.memset(SELB[:], 1.0), writes=[idn])
        sch.do("gp", lambda e: e.affine_select(
            SELA[:], SELA[:], [[-1, 128]], ALU.is_ge, 0.0,
            base=63, channel_multiplier=0), reads=[idn], writes=[idn])
        sch.do("gp", lambda e: e.affine_select(
            SELB[:], SELB[:], [[1, 128]], ALU.is_ge, 0.0,
            base=-64, channel_multiplier=0), reads=[idn], writes=[idn])

    setup_consts()

    PS = [nc.alloc_psum_tensor(f"PS{i}", [128, 512], F32) for i in range(7)]
    psb = [Buf(f"PS{i}") for i in range(7)]
    PT = nc.alloc_psum_tensor("PT", [128, 1024], BF16)
    ptb = Buf("PT")

    taps = {}

    def tap(name, ap, shape, dt, reads):
        if not debug or name in taps:
            return
        h = nc.dram_tensor("dbg_" + name, shape, dt, kind="ExternalOutput").ap()
        taps[name] = True
        sch.do("gdma", lambda e: e.dma_start(out=h, in_=ap), reads=reads)

    def ln_stats(x_ap, x_buf, ncols, par, x_bufs=None):
        b0 = 16 * par
        stb = stbs[par]

        def S(i):
            return STATS[:, b0 + i:b0 + i + 1]

        sch.do("dve", lambda e: e.reduce_sum(S(0), x_ap, axis=AX.X),
               reads=([x_buf] if x_bufs is None else x_bufs), writes=[stb])
        nsl = ncols // 512
        a0 = 16 * par
        sch.do("dve", lambda e, nsl=nsl, a0=a0: e.memset(
            STATS2[:, a0:a0 + nsl], 0.0),
            writes=[stb2s[a0 + s] for s in range(nsl)])
        for s in range(nsl):
            sch.do("act", lambda e, s=s, a0=a0: e.activation(
                SQSCR[:, (s % 2) * 512:(s % 2) * 512 + 512],
                x_ap[:, s * 512:(s + 1) * 512], AF.Square,
                accum_out=STATS2[:, a0 + s:a0 + s + 1]),
                reads=[x_buf if x_bufs is None else x_bufs[s]],
                writes=[sqbs[s % 2], stb2s[a0 + s]])
        sch.do("dve", lambda e, nsl=nsl, a0=a0: e.reduce_sum(
            S(1), STATS2[:, a0:a0 + nsl], axis=AX.X),
            reads=[stb2s[a0 + s] for s in range(nsl)], writes=[stb])
        inv = 1.0 / ncols
        sch.do("dve", lambda e: e.tensor_single_scalar(
            S(0), S(0), inv, ALU.mult), reads=[stb], writes=[stb])
        sch.do("dve", lambda e: e.tensor_tensor(
            S(2), S(0), S(0), ALU.mult), reads=[stb], writes=[stb])
        sch.do("dve", lambda e: e.tensor_scalar(
            S(1), S(1), inv, 0.0, ALU.mult, ALU.add), reads=[stb], writes=[stb])
        sch.do("dve", lambda e: e.tensor_tensor(
            S(1), S(1), S(2), ALU.subtract), reads=[stb], writes=[stb])
        sch.do("act", lambda e: e.activation(S(6), S(1), AF.Sqrt,
                                             bias=c(S_EPS), scale=1.0),
               reads=[stb, cb], writes=[stb])
        sch.do("dve", lambda e: e.reciprocal(S(3), S(6)),
               reads=[stb], writes=[stb])
        sch.do("dve", lambda e: e.tensor_tensor(S(6), S(3), S(3), ALU.mult),
               reads=[stb], writes=[stb])
        sch.do("dve", lambda e: e.tensor_single_scalar(S(7), S(1), EPS,
                                                       ALU.add),
               reads=[stb], writes=[stb])
        sch.do("dve", lambda e: e.tensor_tensor(S(6), S(6), S(7), ALU.mult),
               reads=[stb], writes=[stb])
        sch.do("dve", lambda e: e.tensor_scalar(S(6), S(6), -0.5, 1.5,
                                                ALU.mult, ALU.add),
               reads=[stb], writes=[stb])
        sch.do("dve", lambda e: e.tensor_tensor(S(3), S(3), S(6), ALU.mult),
               reads=[stb], writes=[stb])

    def split_hilo(hi_ap, lo_ap, src_ap, src_buf, dst_buf):
        sch.do("act", lambda e: e.activation(hi_ap, src_ap, AF.Copy),
               reads=[src_buf], writes=[dst_buf])
        sch.do("dve", lambda e: e.scalar_tensor_tensor(
            lo_ap, src_ap, 0.0, hi_ap, ALU.add, ALU.subtract),
            reads=[src_buf, dst_buf], writes=[dst_buf])

    def attention_phase(kind, src_dram, dst_dram, seq, tag):
        es = ExitStack()
        sb = lambda n, shp, dt: es.enter_context(nc.sbuf_tensor(tag + n, shp, dt))
        XT = sb("XT", [128, TT * D], F32)
        xtb = [Buf(f"XT{i}") for i in range(TT)]
        YHI = sb("YHI", [128, 8 * T], BF16)
        YLO = sb("YLO", [128, 8 * T], BF16)
        ybs = [Buf(f"Y{i}") for i in range(TT)]
        KHI = sb("KHI", [128, T], BF16)
        KLO = sb("KLO", [128, T], BF16)
        kb = Buf("K")
        VEXA = sb("VEXA", [128, 16 * 66], F32)
        VEXB = sb("VEXB", [128, 16 * 66], F32)
        vb = Buf("V")
        ONT = sb("ONT", [128, QW], F32)
        RCQ = sb("RCQ", [128, 2], F32)
        rcq = Buf("RCQ")
        VSTG = sb("VSTG", [128, 512], F32)
        vsb = Buf("VS")
        QHI = sb("QHI", [128, QW], BF16)
        QLO = sb("QLO", [128, QW], BF16)
        qb = Buf("Q")
        ETW = QW if kind == "a" else QWB
        NE = 3 if kind == "a" else 4
        ET = sb("ET", [128, NE * ETW], F32)
        eb = [Buf(f"E{i}") for i in range(NE)]
        ONR = sb("ONR", [128, QW], F32)
        ONHI = sb("ONHI", [128, QW], BF16)
        ONLO = sb("ONLO", [128, QW], BF16)
        onb = Buf("ON")
        RCPA = sb("RCPA", [1, 512], F32)
        RCPB_ = sb("RCPB_", [1, 512], F32)
        rcb = Buf("RCP")
        RCPB = sb("RCPBT", [128, 512], F32)
        rcpb = Buf("RCPB")
        WSTG = sb("WSTG", [128, 1024], F32)
        wsb = Buf("WS")
        WQH = sb("WQH", [128, 1024], BF16)
        WQL = sb("WQL", [128, 1024], BF16)
        wqb = Buf("WQ")
        WKH = sb("WKH", [128, 1024], BF16)
        WKL = sb("WKL", [128, 1024], BF16)
        wkb = Buf("WK")
        WVH, WVL, wvb = WKH, WKL, wkb
        WOH = sb("WOH", [128, 1024], BF16)
        WOL = sb("WOL", [128, 1024], BF16)
        wob = Buf("WO")
        if kind == "a":
            BORH = sb("BORH", [1, 1024], BF16)
            BORL = sb("BORL", [1, 1024], BF16)
        bob = Buf("BO")
        YTMP = sb("YTMP", [128, 2 * D], F32)
        ytbs = [Buf("YT0"), Buf("YT1")]
        if kind == "b":
            MSK = sb("MSK", [128, 6 * QWB], BF16)
            mkb = Buf("MSK")
        sch.set_fence()

        sch.do("gp", lambda e: e.memset(VEXA[:, :], 0.0), writes=[vb])
        sch.do("gp", lambda e: e.memset(VEXB[:, :], 0.0), writes=[vb])
        sch.do("gp", lambda e: e.memset(
            VEXA[:, :].rearrange("p (c n) -> p c n", n=66)[:, :, 64:65], 1.0),
            reads=[vb], writes=[vb])
        sch.do("gp", lambda e: e.memset(
            VEXB[:, :].rearrange("p (c n) -> p c n", n=66)[:, :, 64:65], 1.0),
            reads=[vb], writes=[vb])
        qw = QW if kind == "a" else QWB
        nqt = T // qw
        Wq = W["aWq"] if kind == "a" else W["bWq"]
        Wk = W["aWk"] if kind == "a" else W["bWk"]
        Wv = W["aWv"] if kind == "a" else W["bWv"]
        Wo = W["aWo"] if kind == "a" else W["bWo"]

        def mask_t(i):
            return MSK[:, i * QWB:(i + 1) * QWB]

        if kind == "b":
            specs = [(0, 1, -128, -1), (1, 1, 0, -1),
                     (2, -1, 127, 1), (3, -1, -1, 1)]
            for idx, fc, base, cm in specs:
                sch.do("gp", lambda e, i=idx: e.memset(mask_t(i), 0.0),
                       writes=[mkb])
                sch.do("gp", lambda e, i=idx, fc=fc, base=base, cm=cm:
                       e.affine_select(mask_t(i), mask_t(i), [[fc, QWB]],
                                       ALU.is_ge, -1e38, base=base,
                                       channel_multiplier=cm),
                       reads=[mkb], writes=[mkb])
            for src, dst in ((2, 4), (3, 5)):
                sch.do("gp", lambda e, s=src, d=dst: e.tensor_copy(
                    mask_t(d), mask_t(s)), reads=[mkb], writes=[mkb])
                sch.do("gp", lambda e, d=dst: e.memset(
                    mask_t(d)[0:SINK, :], 0.0), reads=[mkb], writes=[mkb])

        def load_w_cols(dram, col0, ncol, replicate, hi_t, lo_t, dbuf,
                        eng="sdma"):
            n = 2 * ncol if replicate else ncol
            src = dram[:, col0:col0 + ncol].rearrange(
                "(c p) n -> p c n", p=128)
            outv = WSTG[:, 0:8 * n].rearrange("p (c n) -> p c n", c=8)
            if replicate:
                sch.do(eng, lambda e: e.dma_start(
                    out=outv[:, :, 0:ncol], in_=src), writes=[wsb])
                sch.do(eng, lambda e: e.dma_start(
                    out=outv[:, :, ncol:2 * ncol], in_=src),
                    reads=[wsb], writes=[wsb])
            else:
                sch.do(eng, lambda e: e.dma_start(out=outv, in_=src),
                       writes=[wsb])
            split_hilo(hi_t[:, 0:8 * n], lo_t[:, 0:8 * n], WSTG[:, 0:8 * n],
                       wsb, dbuf)

        for i in range(TT):
            sch.do("gdma", lambda e, i=i, seq=seq: e.dma_start(
                out=XT[:, i * D:(i + 1) * D],
                in_=src_dram[seq, i * 128:(i + 1) * 128, :]), writes=[xtb[i]])
            par = i % 2
            yt = YTMP[:, par * D:(par + 1) * D]
            ln_stats(XT[:, i * D:(i + 1) * D], xtb[i], D, par)
            sch.do("dve", lambda e, i=i, par=par, yt=yt: e.tensor_scalar(
                yt, XT[:, i * D:(i + 1) * D],
                STATS[:, 16 * par:16 * par + 1],
                STATS[:, 16 * par + 3:16 * par + 4],
                ALU.subtract, ALU.mult),
                reads=[xtb[i], stbs[par]], writes=[ytbs[par]])
            if tag == "A0_" and i == 0:
                tap("xt0", XT[:, 0:D], [128, D], F32, [xtb[0]])
                tap("ytmp0", YTMP[:, 0:D], [128, D], F32, [ytbs[0]])
                tap("stats0", STATS[:, 0:4], [128, 4], F32, [stbs[0]])
            for half in range(2):
                pb, pst = psb[half], PS[half]
                def tp(e, half=half, pst=pst, yt=yt):
                    ins = []
                    for j in range(4):
                        ch = half * 4 + j
                        ins.append(e.matmul(
                            pst[:, j * 128:(j + 1) * 128],
                            yt[:, ch * 128:(ch + 1) * 128], ID32[:],
                            start=(j == 0), stop=(j == 3), is_transpose=True))
                    return ins
                sch.do("pe", tp, reads=[ytbs[par], idn], writes=[pb])
                yh = YHI[:, :].rearrange("p (c t) -> p c t", c=8)[
                    :, half * 4:half * 4 + 4, i * 128:(i + 1) * 128]
                yl = YLO[:, :].rearrange("p (c t) -> p c t", c=8)[
                    :, half * 4:half * 4 + 4, i * 128:(i + 1) * 128]
                pv = pst[:, :].rearrange("p (c t) -> p c t", c=4)
                split_hilo(yh, yl, pv, pb, ybs[i])

        if tag == "A0_":
            tap("yhi", YHI[:, :], [128, 8 * T], BF16, list(ybs))
            tap("ylo", YLO[:, :], [128, 8 * T], BF16, list(ybs))
        if trunc == "phaseA":
            es.close()
            sch.set_fence()
            return
        pend_tails = []

        def flush_tails():
            while pend_tails:
                pend_tails.pop(0)()

        for hp in range(NH // 2):
            flush_tails()
            if kind == "a":
                build_kv = True
                kcol = hp * 128
                vcol = hp * 128
                repl = False
            else:
                g = hp // 2
                build_kv = (hp % 2 == 0)
                kcol = g * 64
                vcol = g * 64
                repl = True
            if build_kv:
                load_w_cols(Wk, kcol, 64 if repl else 128, repl, WKH, WKL,
                            wkb, eng="adma")
                for kt in range(T // 512):
                    pb, pst = psb[4 + (kt % 2)], PS[4 + (kt % 2)]
                    def mk(e, kt=kt, pst=pst):
                        ins = []
                        for ch in range(8):
                            a_hi = WKH[:, ch * 128:(ch + 1) * 128]
                            a_lo = WKL[:, ch * 128:(ch + 1) * 128]
                            b_hi = YHI[:, ch * T + kt * 512:
                                       ch * T + kt * 512 + 512]
                            b_lo = YLO[:, ch * T + kt * 512:
                                       ch * T + kt * 512 + 512]
                            for a, bb in ((a_hi, b_hi), (a_hi, b_lo),
                                          (a_lo, b_hi)):
                                ins.append(e.matmul(
                                    pst[:], a, bb,
                                    start=(ch == 0 and a is a_hi and bb is b_hi),
                                    stop=(ch == 7 and a is a_lo)))
                        return ins
                    sch.do("pe", mk,
                           reads=[wkb] + ybs[kt * 4:kt * 4 + 4],
                           writes=[pb])
                    split_hilo(KHI[:, kt * 512:(kt + 1) * 512],
                               KLO[:, kt * 512:(kt + 1) * 512], pst[:], pb, kb)
                load_w_cols(Wv, vcol, 64 if repl else 128, repl, WVH, WVL, wvb)
                for vt in range(T // 512):
                    pb, pst = psb[4 + (vt % 2)], PS[4 + (vt % 2)]
                    def mv(e, vt=vt, pst=pst):
                        ins = []
                        for ch in range(8):
                            a_hi = WVH[:, ch * 128:(ch + 1) * 128]
                            a_lo = WVL[:, ch * 128:(ch + 1) * 128]
                            b_hi = YHI[:, ch * T + vt * 512:
                                       ch * T + vt * 512 + 512]
                            b_lo = YLO[:, ch * T + vt * 512:
                                       ch * T + vt * 512 + 512]
                            for a, bb in ((a_hi, b_hi), (a_hi, b_lo),
                                          (a_lo, b_hi)):
                                ins.append(e.matmul(
                                    pst[:], a, bb,
                                    start=(ch == 0 and a is a_hi and bb is b_hi),
                                    stop=(ch == 7 and a is a_lo)))
                        return ins
                    sch.do("pe", mv,
                           reads=[wvb] + ybs[vt * 4:vt * 4 + 4],
                           writes=[pb])
                    sch.do("act", lambda e, vt=vt, pst=pst: e.activation(
                        VSTG[:, 0:512], pst[:],
                        AF.Copy), reads=[pb], writes=[vsb])
                    pb2, pst2 = psb[(vt + 1) % 2], PS[(vt + 1) % 2]
                    def tpv(e, vt=vt, pst2=pst2):
                        ins = []
                        for j in range(4):
                            ins.append(e.matmul(
                                pst2[:, j * 128:(j + 1) * 128],
                                VSTG[:, j * 128:(j + 1) * 128],
                                ID32[:], start=(j == 0), stop=(j == 3),
                                is_transpose=True))
                        return ins
                    sch.do("pe", tpv, reads=[vsb, idn], writes=[pb2])
                    va = VEXA[:, :].rearrange("p (c n) -> p c n", n=66)[
                        :, vt * 4:vt * 4 + 4, 0:64]
                    vbv = VEXB[:, :].rearrange("p (c n) -> p c n", n=66)[
                        :, vt * 4:vt * 4 + 4, 0:64]
                    pa = pst2[:, :].rearrange("p (c n) -> p c n", n=128)
                    sch.do("act", lambda e, va=va, pa=pa: e.activation(
                        va, pa[:, :, 0:64], AF.Copy), reads=[pb2], writes=[vb])
                    sch.do("act", lambda e, vbv=vbv, pa=pa: e.activation(
                        vbv, pa[:, :, 64:128], AF.Copy), reads=[pb2], writes=[vb])

            if tag == "A0_" and hp == 0:
                tap("khi", KHI[:, :], [128, T], BF16, [kb])
                tap("klo", KLO[:, :], [128, T], BF16, [kb])
                tap("vexa", VEXA[:, :], [128, 16 * 128], F32, [vb])
                tap("vexb", VEXB[:, :], [128, 16 * 128], F32, [vb])
            load_w_cols(Wq, hp * 128, 128, False, WQH, WQL, wqb)
            sch.do("sdma", lambda e, hp=hp: e.dma_start(
                out=WSTG[:, 0:1024], in_=Wo[hp * 128:(hp + 1) * 128, :]),
                writes=[wsb])
            split_hilo(WOH[:, 0:1024], WOL[:, 0:1024], WSTG[:, 0:1024],
                       wsb, wob)
            if kind == "a" and hp == 0:
                sch.do("sdma", lambda e: e.dma_start(
                    out=WSTG[0:1, 0:1024], in_=W["abo"].rearrange("(a d) -> a d", a=1)),
                    reads=[wsb], writes=[wsb])
                split_hilo(BORH[0:1, :], BORL[0:1, :], WSTG[0:1, 0:1024],
                           wsb, bob)

            for qt in range(nqt):
                q0 = qt * qw
                flush_tails()
                pbq, psq = psb[5], PS[5]
                def mq(e, q0=q0):
                    ins = []
                    for ch in range(8):
                        a_hi = WQH[:, ch * 128:(ch + 1) * 128]
                        a_lo = WQL[:, ch * 128:(ch + 1) * 128]
                        b_hi = YHI[:, ch * T + q0:ch * T + q0 + qw]
                        b_lo = YLO[:, ch * T + q0:ch * T + q0 + qw]
                        for a, bb in ((a_hi, b_hi), (a_hi, b_lo),
                                      (a_lo, b_hi)):
                            ins.append(e.matmul(
                                psq[:, 0:qw], a, bb,
                                start=(ch == 0 and a is a_hi and bb is b_hi),
                                stop=(ch == 7 and a is a_lo)))
                    return ins
                sch.do("pe", mq,
                       reads=[wqb] + ybs[(q0 // 128):(q0 + qw) // 128],
                       writes=[pbq])
                split_hilo(QHI[:, 0:qw], QLO[:, 0:qw], psq[:, 0:qw], pbq, qb)
                if tag == "A0_" and hp == 0 and qt == 0:
                    tap("qhi", QHI[:, 0:qw], [128, qw], BF16, [qb])
                    tap("qlo", QLO[:, 0:qw], [128, qw], BF16, [qb])

                if kind == "a":
                    chunks = [(None, kc) for kc in range(TT)]
                else:
                    i0 = q0
                    chunks = []
                    for j0 in range(max(0, i0 - WIN), i0 + qw, 128):
                        d0 = i0 - j0
                        if d0 == 256:
                            cls = 5 if j0 == 0 else 3
                        elif d0 == 128:
                            cls = 4 if j0 == 0 else 2
                        elif d0 == 0:
                            cls = 1
                        elif d0 == -128:
                            cls = 0
                        else:
                            cls = None
                        chunks.append((cls, j0 // 128))
                    if i0 - WIN > 0:
                        chunks.append(("sink", 0))

                oa_, psoa = psb[2], PS[2]
                ob2_, psob = psb[6], PS[6]
                ncht = len(chunks)
                pend_pv = []
                for ci, (cls, kc) in enumerate(chunks):
                    is_sink = (cls == "sink")
                    krows = SINK if is_sink else 128
                    for h in range(2):
                        sbk = 3 + ((2 * ci + h) % 3)
                        hb, psh = psb[sbk], PS[sbk]
                        r0 = h * 64
                        def msc(e, kc=kc, krows=krows, r0=r0, psh=psh):
                            ins = []
                            a_hi = KHI[r0:r0 + 64,
                                       kc * 128:kc * 128 + krows]
                            a_lo = KLO[r0:r0 + 64,
                                       kc * 128:kc * 128 + krows]
                            b_hi = QHI[r0:r0 + 64, 0:qw]
                            b_lo = QLO[r0:r0 + 64, 0:qw]
                            for i_, (a, bb) in enumerate(
                                    ((a_hi, b_hi), (a_hi, b_lo),
                                     (a_lo, b_hi))):
                                ins.append(e.matmul(
                                    psh[0:krows, 0:qw], a, bb,
                                    start=(i_ == 0), stop=(i_ == 2),
                                    tile_position=(r0, 0)))
                            return ins
                        sch.do("pe", msc, reads=[kb, qb], writes=[hb])
                        if kind == "b" and cls is not None and not is_sink:
                            sch.do("dve", lambda e, psh=psh, cls=cls,
                                   krows=krows: e.tensor_tensor(
                                       psh[0:krows, 0:qw], psh[0:krows, 0:qw],
                                       mask_t(cls)[0:krows, :], ALU.add),
                                   reads=[hb, mkb], writes=[hb])
                        ei = (2 * ci + h) % NE
                        et = ET[:, ei * ETW:ei * ETW + qw]
                        sch.do("act", lambda e, psh=psh, et=et, krows=krows:
                               e.activation(et[0:krows, :],
                                            psh[0:krows, 0:qw], AF.Exp,
                                            bias=0.0,
                                            scale=CB[0:krows,
                                                     S_SCALE:S_SCALE + 1]),
                               reads=[hb, cb], writes=[eb[ei]])
                        if tag == "A0_" and hp == 0 and qt == 0 and ci == 0:
                            tap(f"e{h}", et[0:krows, :], [krows, qw], F32,
                                [eb[ei]])
                        vsl = (VEXA if h == 0 else VEXB)[
                            :, kc * 66:kc * 66 + 65]
                        pv_first = (ci == 0)
                        pv_last = (ci == ncht - 1)

                        def dopv(vsl=vsl, et=et, krows=krows,
                                 pv_first=pv_first, pv_last=pv_last,
                                 pvb=(psoa if h == 0 else psob),
                                 ebx=eb[ei],
                                 obx=(oa_ if h == 0 else ob2_)):
                            def f(e):
                                ins = []
                                for qs_ in range(qw // 128):
                                    ins.append(e.matmul(
                                        pvb[:, qs_ * 66:qs_ * 66 + 65],
                                        et[0:krows,
                                           qs_ * 128:(qs_ + 1) * 128],
                                        vsl[0:krows, :],
                                        start=(pv_first and qs_ == 0),
                                        stop=(pv_last and
                                              qs_ == qw // 128 - 1)))
                                return ins
                            sch.do("pe", f, reads=[ebx, vb], writes=[obx])
                        pend_pv.append(dopv)
                        if len(pend_pv) > 2:
                            pend_pv.pop(0)()
                if True:
                    while pend_pv:
                        pend_pv.pop(0)()

                for qs_ in range(qw // 128):
                    for h in range(2):
                        ob_ = oa_ if h == 0 else ob2_
                        pso_ = psoa if h == 0 else psob
                        if kind == "b":
                            sch.do("dve", lambda e, qs_=qs_, h=h, pso_=pso_:
                                   e.tensor_single_scalar(
                                       RCQ[:, h:h + 1],
                                       pso_[:, qs_ * 66 + 64:qs_ * 66 + 65],
                                       TINY, ALU.max),
                                   reads=[ob_], writes=[rcq])
                            sch.do("dve", lambda e, h=h: e.reciprocal(
                                RCQ[:, h:h + 1], RCQ[:, h:h + 1]),
                                reads=[rcq], writes=[rcq])
                        else:
                            sch.do("dve", lambda e, qs_=qs_, h=h, pso_=pso_:
                                   e.reciprocal(
                                       RCQ[:, h:h + 1],
                                       pso_[:, qs_ * 66 + 64:qs_ * 66 + 65]),
                                   reads=[ob_], writes=[rcq])
                        sch.do("act", lambda e, qs_=qs_, h=h, pso_=pso_:
                               e.activation(
                                   ONT[:, qs_ * 128 + h * 64:
                                       qs_ * 128 + h * 64 + 64],
                                   pso_[:, qs_ * 66:qs_ * 66 + 64],
                                   AF.Copy, scale=RCQ[:, h:h + 1]),
                               reads=[ob_, rcq], writes=[onb])
                def dotail(qt=qt, q0=q0, hp=hp):
                    for qs_ in range(qw // 128):
                        pbT, psT = psb[qs_ % 2], PS[qs_ % 2]
                        sch.do("pe", lambda e, qs_=qs_, psT=psT: e.matmul(
                            psT[:, 0:128],
                            ONT[:, qs_ * 128:(qs_ + 1) * 128], ID32[:],
                            start=True, stop=True, is_transpose=True),
                            reads=[onb, idn], writes=[pbT])
                        split_hilo(ONHI[:, qs_ * 128:(qs_ + 1) * 128],
                                   ONLO[:, qs_ * 128:(qs_ + 1) * 128],
                                   psT[:, 0:128], pbT, onb)
                    for tg in range(qw // 128):
                        ti = (q0 + tg * 128) // 128
                        for half in range(2):
                            wb2, psw = psb[half], PS[half]
                            def mwo(e, tg=tg, half=half, hp=hp, psw=psw):
                                ins = []
                                a_hi = ONHI[:, tg * 128:(tg + 1) * 128]
                                a_lo = ONLO[:, tg * 128:(tg + 1) * 128]
                                b_hi = WOH[:, half * 512:(half + 1) * 512]
                                b_lo = WOL[:, half * 512:(half + 1) * 512]
                                for i_, (a, bb) in enumerate(
                                        ((a_hi, b_hi), (a_hi, b_lo),
                                         (a_lo, b_hi))):
                                    ins.append(e.matmul(
                                        psw[:], a, bb, start=(i_ == 0),
                                        stop=(i_ == 2 and not
                                              (kind == "a" and hp == 0))))
                                if kind == "a" and hp == 0:
                                    ins.append(e.matmul(
                                        psw[:], ONES1B[:, :],
                                        BORH[0:1, half * 512:(half + 1) * 512],
                                        start=False, stop=False))
                                    ins.append(e.matmul(
                                        psw[:], ONES1B[:, :],
                                        BORL[0:1, half * 512:(half + 1) * 512],
                                        start=False, stop=True))
                                return ins
                            sch.do("pe", mwo, reads=[onb, wob, bob, idn],
                                   writes=[wb2])
                            sch.do("dve", lambda e, ti=ti, half=half, psw=psw:
                                   e.tensor_tensor(
                                       XT[:, ti * D + half * 512:
                                          ti * D + half * 512 + 512],
                                       XT[:, ti * D + half * 512:
                                          ti * D + half * 512 + 512],
                                       psw[:], ALU.add),
                                   reads=[wb2, xtb[ti]], writes=[xtb[ti]])
                pend_tails.append(dotail)

        flush_tails()
        for i in range(TT):
            sch.do("gdma", lambda e, i=i, seq=seq: e.dma_start(
                out=dst_dram[seq, i * 128:(i + 1) * 128, :],
                in_=XT[:, i * D:(i + 1) * D]), reads=[xtb[i]])
        es.close()
        sch.set_fence()

    def ffn_phase(src_dram, dst_dram, first, tag):
        es = ExitStack()
        sb = lambda n, shp, dt: es.enter_context(nc.sbuf_tensor(tag + n, shp, dt))
        W1Q = sb("W1Q", [128, 8 * HF], FP8)
        w1b = Buf("W1Q")
        W2Q = sb("W2Q", [128, 32 * D], FP8)
        w2b = Buf("W2Q")
        H3 = sb("H3", [128, HF], F32)
        h3bs = [Buf(f"H3_{i}") for i in range(8)]
        H4 = H3
        h4bs = h3bs
        G1S = sb("G1S", [128, D], F32)
        g1b = Buf("G1S")
        H1S = sb("H1S", [128, D], F32)
        h1b = Buf("H1S")
        WBC = sb("WBC", [128, HF], F32)
        BBC = sb("BBC", [128, HF], F32)
        B1BC = sb("B1BC", [128, HF], F32)
        W1BC = sb("W1BC", [128, D], F32)
        B1LBC = sb("B1LBC", [128, D], F32)
        B2BC = sb("B2BC", [128, D], F32)
        bcb = Buf("BC")
        G3S = sb("G3S", [128, 1024], F32)
        g3bs = [Buf("G3S0"), Buf("G3S1")]
        US = sb("US", [128, 1024], F32)
        usbs = [Buf("US0"), Buf("US1")]
        UHS = sb("UHS", [128, 1024], F32)
        uhbs = [Buf("UHS0"), Buf("UHS1")]
        TQ = sb("TQ", [128, 1024], F32)
        tqbs = [Buf("TQ0"), Buf("TQ1")]
        K1N = sb("K1N", [128, D], BF16)
        K1F = sb("K1F", [128, D], BF16)
        k1b = Buf("K1")
        k1fb = Buf("K1F")
        K2N = sb("K2N", [128, HF], BF16)
        k2bs = [Buf(f"K2_{i}") for i in range(8)]
        K2F = sb("K2F", [128, 2048], BF16)
        k2fbs = [Buf("K2F0"), Buf("K2F1")]
        WSTG = sb("WSTGF", [128, 1024], F32)
        wsb = Buf("WSF")
        XIN = sb("XIN", [128, 2 * D], F32)
        xib = [Buf("XI0"), Buf("XI1")]
        XOUT = sb("XOUT", [128, D], F32)
        xob = Buf("XO")
        YT = sb("YTF", [128, D], F32)
        ytb = Buf("YTF")
        sch.set_fence()

        def quant_tile(src_ap, src_bufs, ncols, dst_t, dst_bufs, s_ap, s_buf):
            nsl = ncols // 512
            for s in range(nsl):
                sl = slice(s * 512, (s + 1) * 512)
                sbuf = src_bufs[min(s, len(src_bufs) - 1)]
                dbuf = dst_bufs[min(s, len(dst_bufs) - 1)]
                us = US[:, (s % 2) * 512:(s % 2) * 512 + 512]
                g3 = G3S[:, (s % 2) * 512:(s % 2) * 512 + 512]
                ts_ = TQ[:, (s % 2) * 512:(s % 2) * 512 + 512]
                sch.do("act", lambda e, sl=sl, us=us: e.activation(
                    us, src_ap[:, sl], AF.Identity, bias=c(S_C1), scale=s_ap),
                    reads=[sbuf, s_buf, cb], writes=[usbs[s % 2]])
                sch.do("act", lambda e, sl=sl, g3=g3: e.activation(
                    g3, src_ap[:, sl], AF.Identity, bias=c(S_C3), scale=s_ap),
                    reads=[sbuf, s_buf, cb], writes=[g3bs[s % 2]])
                sch.do("dve", lambda e, us=us, ts_=ts_: e.tensor_scalar(
                    ts_, us, c(S_APN), c(S_KNN), ALU.mult, ALU.add),
                    reads=[usbs[s % 2], cb], writes=[tqbs[s % 2]])
                sch.do("dve", lambda e, sl=sl, g3=g3, ts_=ts_:
                       e.scalar_tensor_tensor(
                           dst_t[:, sl], g3, c(S_BPN), ts_,
                           ALU.mult, ALU.add),
                       reads=[g3bs[s % 2], tqbs[s % 2], cb], writes=[dbuf])

        def quantize_weights():
            for wd, nrow, wcols, dst_t, dbuf, spill, qcol, tile_cols in (
                    (W["fw1"], HF, D, W1Q, w1b, w1q_sp, 0, D),
                    (W["fw2"], D, HF, W2Q, w2b, w2q_sp, 2, 1024)):
                ntr = nrow // 128
                nct = wcols // tile_cols
                first_acc = [True]
                for it in range(ntr):
                    for jt in range(nct):
                        sch.do("sdma", lambda e, it=it, jt=jt, wd=wd, tile_cols=tile_cols:
                               e.dma_start(
                            out=WSTG[:, 0:tile_cols],
                            in_=wd[it * 128:(it + 1) * 128,
                                   jt * tile_cols:(jt + 1) * tile_cols]),
                            writes=[wsb])
                        sch.do("dve", lambda e, tile_cols=tile_cols: e.reduce_sum(
                            STATS[:, 6:7], WSTG[:, 0:tile_cols], axis=AX.X,
                            apply_absolute_value=True),
                            reads=[wsb], writes=[stbs[0]])
                        fa = first_acc[0]
                        first_acc[0] = False
                        if fa:
                            sch.do("dve", lambda e: e.tensor_copy(
                                STATS[:, 5:6], STATS[:, 6:7]),
                                reads=[stbs[0]], writes=[stbs[0]])
                        else:
                            sch.do("dve", lambda e: e.tensor_tensor(
                                STATS[:, 5:6], STATS[:, 5:6], STATS[:, 6:7],
                                ALU.add), reads=[stbs[0]], writes=[stbs[0]])
                pb, pst = psb[0], PS[0]
                sch.do("pe", lambda e: e.matmul(
                    pst[0:1, 0:1], ONES1[:, 0:1], STATS[:, 5:6],
                    start=True, stop=True), reads=[stbs[0], idn], writes=[pb])
                sch.do("dve", lambda e: e.tensor_copy(
                    STATS2[0:1, 8:9], pst[0:1, 0:1]), reads=[pb],
                    writes=[stb2s[8]])
                sch.do("pe", lambda e: e.matmul(
                    pst[:, 0:1], ONESR[:, :], STATS2[0:1, 8:9],
                    start=True, stop=True), reads=[stb2s[8], idn], writes=[pb])
                mi = 1.0 / float(nrow * wcols)
                sch.do("dve", lambda e, mi=mi, qcol=qcol: e.tensor_scalar(
                    QS[:, qcol:qcol + 1], pst[:, 0:1], mi, 0.0,
                    ALU.mult, ALU.add), reads=[pb], writes=[qs])
                sch.do("dve", lambda e, qcol=qcol: e.tensor_tensor(
                    QS[:, qcol:qcol + 1], QS[:, qcol:qcol + 1], c(S_EPS),
                    ALU.max), reads=[qs, cb], writes=[qs])
                sch.do("dve", lambda e, qcol=qcol: e.reciprocal(
                    QS[:, qcol + 1:qcol + 2], QS[:, qcol:qcol + 1]),
                    reads=[qs], writes=[qs])
                for it in range(ntr):
                    for jt in range(nct):
                        sch.do("sdma", lambda e, it=it, jt=jt, wd=wd, tile_cols=tile_cols:
                               e.dma_start(
                            out=WSTG[:, 0:tile_cols],
                            in_=wd[it * 128:(it + 1) * 128,
                                   jt * tile_cols:(jt + 1) * tile_cols]),
                            writes=[wsb])
                        quant_tile(WSTG, [wsb], tile_cols, K2N, [k2bs[0]],
                                   QS[:, qcol + 1:qcol + 2], qs)
                        sch.do("dve", lambda e, tile_cols=tile_cols: e.tensor_scalar(
                            K2N[:, 0:tile_cols], K2N[:, 0:tile_cols],
                            1.0, -1.0, ALU.min, ALU.max),
                            reads=[k2bs[0]], writes=[k2bs[0]])
                        ncc = tile_cols // 128
                        for c8 in range(0, ncc, 8):
                            n8 = min(8, ncc - c8)
                            def tpq(e, c8=c8, n8=n8):
                                ins = []
                                for j in range(n8):
                                    ins.append(e.matmul(
                                        PT[:, j * 128:(j + 1) * 128],
                                        K2N[:, (c8 + j) * 128:
                                            (c8 + j) * 128 + 128],
                                        IDBF[:], start=(j == 0),
                                        stop=(j == n8 - 1),
                                        is_transpose=True))
                                return ins
                            sch.do("pe", tpq, reads=[k2bs[0], idn], writes=[ptb])
                            def ev(e, it=it, jt=jt, c8=c8, n8=n8,
                                   dst_t=dst_t, ncc=ncc, nrow=nrow):
                                dv = dst_t[:, :].rearrange(
                                    "p (c n) -> p c n", n=nrow)
                                ch0 = jt * ncc + c8
                                return e.tensor_copy(
                                    dv[:, ch0:ch0 + n8,
                                       it * 128:it * 128 + 128],
                                    PT[:, 0:n8 * 128].rearrange(
                                        "p (c n) -> p c n", c=n8))
                            sch.do("dve", ev, reads=[ptb], writes=[dbuf])
                sch.do("gdma", lambda e, dst_t=dst_t, spill=spill:
                       e.dma_start(out=spill, in_=dst_t[:, :]),
                       reads=[dbuf])

        if first:
            quantize_weights()
        else:
            for dst_t, dbuf, spill in ((W1Q, w1b, w1q_sp),
                                       (W2Q, w2b, w2q_sp)):
                sch.do("gdma", lambda e, dst_t=dst_t, spill=spill:
                       e.dma_start(out=dst_t[:, :], in_=spill),
                       writes=[dbuf])

        def bcast(dst_t, row_dram, ncols):
            sch.do("sdma", lambda e: e.dma_start(
                out=H3[0:1, 0:ncols], in_=row_dram.rearrange("(a d) -> a d", a=1)),
                writes=list(h3bs))
            for s0 in range(0, ncols, 512):
                pb, pst = psb[0], PS[0]
                sch.do("pe", lambda e, s0=s0: e.matmul(
                    pst[:], ONESR[:, :], H3[0:1, s0:s0 + 512],
                    start=True, stop=True), reads=list(h3bs), writes=[pb])
                sch.do("act", lambda e, s0=s0: e.activation(
                    dst_t[:, s0:s0 + 512], pst[:], AF.Copy),
                    reads=[pb], writes=[bcb])

        bcast(WBC, W["fln2w"], HF)
        bcast(BBC, W["fln2b"], HF)
        bcast(B1BC, W["fb1"], HF)
        bcast(W1BC, W["fln1w"], D)
        bcast(B1LBC, W["fln1b"], D)
        bcast(B2BC, W["fb2"], D)

        def amax_partial(src_ap, src_buf, s, par):
            m0 = 32 + 8 * par
            sch.do("dve", lambda e, s=s, m0=m0: e.reduce_max(
                STATS2[:, m0 + s:m0 + s + 1],
                src_ap[:, s * 512:(s + 1) * 512], axis=AX.X,
                apply_absolute_value=True),
                reads=[src_buf], writes=[stb2s[m0 + s]])

        def amax_scale(nsl, ds_slot, mw_col, par):
            b0 = 16 * par
            m0 = 32 + 8 * par
            stb = stbs[par]

            def S(i):
                return STATS[:, b0 + i:b0 + i + 1]

            sch.do("dve", lambda e, nsl=nsl, m0=m0: e.reduce_max(
                S(2), STATS2[:, m0:m0 + nsl], axis=AX.X),
                reads=[stb2s[m0 + s] for s in range(nsl)], writes=[stb])
            sch.do("dve", lambda e: e.tensor_tensor(
                S(2), S(2), c(S_EPS), ALU.max), reads=[stb, cb], writes=[stb])
            sch.do("dve", lambda e: e.reciprocal(S(3), S(2)),
                   reads=[stb], writes=[stb])
            sch.do("dve", lambda e: e.tensor_single_scalar(
                S(3), S(3), 127.0, ALU.mult), reads=[stb], writes=[stb])
            sch.do("dve", lambda e, ds_slot=ds_slot, mw_col=mw_col:
                   e.tensor_tensor(S(ds_slot), S(2),
                                   QS[:, mw_col:mw_col + 1], ALU.mult),
                   reads=[stb, qs], writes=[stb])
            sch.do("dve", lambda e, ds_slot=ds_slot: e.tensor_single_scalar(
                S(ds_slot), S(ds_slot), 1.0 / 127.0, ALU.mult),
                reads=[stb], writes=[stb])

        def prelude(gidx):
            seq2, it = divmod(gidx, TT)
            par = gidx % 2

            def mkS(i, par=par):
                return STATS[:, 16 * par + i:16 * par + i + 1]

            stb = stbs[par]
            xsel = gidx % 2
            xi = XIN[:, xsel * D:(xsel + 1) * D]
            xbuf = xib[xsel]
            sch.do("gdma", lambda e, it=it, xi=xi, seq2=seq2:
                   e.dma_start(out=xi, in_=src_dram[
                       seq2, it * 128:(it + 1) * 128, :]), writes=[xbuf])
            ln_stats(xi, xbuf, D, par)
            sch.do("dve", lambda e, xi=xi, s0=mkS(0), s3=mkS(3):
                   e.tensor_scalar(YT[:], xi, s0, s3,
                                   ALU.subtract, ALU.mult),
                   reads=[xbuf, stb], writes=[ytb])
            ln_stats(YT[:], ytb, D, par)
            sch.do("dve", lambda e, s6=mkS(6), s0=mkS(0), s3=mkS(3):
                   e.tensor_tensor(s6, s0, s3, ALU.mult),
                   reads=[stb], writes=[stb])
            sch.do("dve", lambda e, s6=mkS(6): e.tensor_single_scalar(
                s6, s6, -1.0, ALU.mult), reads=[stb], writes=[stb])
            sch.do("act", lambda e, s6=mkS(6), s3=mkS(3): e.activation(
                G1S[:], YT[:], AF.Identity, bias=s6, scale=s3),
                reads=[ytb, stb], writes=[g1b])
            sch.do("dve", lambda e: e.tensor_tensor(
                H1S[:], G1S[:], W1BC[:], ALU.mult),
                reads=[g1b, bcb], writes=[h1b])
            sch.do("dve", lambda e: e.tensor_tensor(
                H1S[:], H1S[:], B1LBC[:], ALU.add),
                reads=[h1b, bcb], writes=[h1b])
            for s in range(2):
                amax_partial(H1S, h1b, s, par)
            amax_scale(2, 9, 0, par)
            quant_tile(H1S, [h1b], D, K1N, [k1b], mkS(3), stb)
            def tk1(e):
                ins = []
                for j in range(8):
                    ins.append(e.matmul(
                        PT[:, j * 128:(j + 1) * 128],
                        K1N[:, j * 128:(j + 1) * 128], IDBF[:],
                        start=(j == 0), stop=(j == 7), is_transpose=True))
                return ins
            sch.do("pe", tk1, reads=[k1b, idn], writes=[ptb])
            sch.do("act", lambda e: e.activation(K1F[:], PT[:, 0:D],
                                                 AF.Copy),
                   reads=[ptb], writes=[k1fb])
            return xi, xbuf, par

        def body(gidx):
            par = gidx % 2
            a0 = 16 * par
            stb = stbs[par]

            def mkS(i, par=par):
                return STATS[:, 16 * par + i:16 * par + i + 1]

            sch.do("dve", lambda e, a0=a0: e.memset(
                STATS2[:, a0:a0 + 16], 0.0),
                writes=[stb2s[a0 + i] for i in range(16)])
            for hs in range(HF // 512):
                pb, pst = psb[hs % 2], PS[hs % 2]
                def m1(e, hs=hs, pst=pst):
                    ins = []
                    for chn in range(8):
                        ins.append(e.matmul(
                            pst[:], K1F[:, chn * 128:(chn + 1) * 128],
                            W1Q[:, chn * HF + hs * 512:
                                     chn * HF + hs * 512 + 512],
                            start=(chn == 0), stop=(chn == 7)))
                    return ins
                sch.do("pe", m1, reads=[k1fb, w1b], writes=[pb])
                uh = UHS[:, (hs % 2) * 512:(hs % 2) * 512 + 512]
                sch.do("dve", lambda e, hs=hs, pst=pst, uh=uh,
                       s9=mkS(9): e.scalar_tensor_tensor(
                           uh, pst[:], s9,
                           B1BC[:, hs * 512:hs * 512 + 512],
                           ALU.mult, ALU.add),
                       reads=[pb, stb, bcb], writes=[uhbs[hs % 2]])
                sch.do("act", lambda e, hs=hs, uh=uh, a0=a0: e.activation(
                    H3[:, hs * 512:hs * 512 + 512], uh, AF.Gelu,
                    accum_out=STATS2[:, a0 + hs:a0 + hs + 1]),
                    reads=[uhbs[hs % 2]],
                    writes=[h3bs[hs], stb2s[a0 + hs]])
                sch.do("act", lambda e, hs=hs, a0=a0: e.activation(
                    SQSCR[:, (hs % 2) * 512:(hs % 2) * 512 + 512],
                    H3[:, hs * 512:(hs + 1) * 512],
                    AF.Square, accum_out=STATS2[:, a0 + 8 + hs:
                                                a0 + 9 + hs]),
                    reads=[h3bs[hs]],
                    writes=[sqbs[hs % 2], stb2s[a0 + 8 + hs]])

        def tail(gidx, xi, xbuf):
            seq2, it = divmod(gidx, TT)
            par = gidx % 2
            a0 = 16 * par
            stb = stbs[par]

            def mkS(i, par=par):
                return STATS[:, 16 * par + i:16 * par + i + 1]

            sch.do("dve", lambda e, a0=a0, s0=mkS(0): e.reduce_sum(
                s0, STATS2[:, a0:a0 + 8], axis=AX.X),
                reads=[stb2s[a0 + i] for i in range(8)], writes=[stb])
            sch.do("dve", lambda e, a0=a0, s1=mkS(1): e.reduce_sum(
                s1, STATS2[:, a0 + 8:a0 + 16], axis=AX.X),
                reads=[stb2s[a0 + 8 + i] for i in range(8)], writes=[stb])
            inv = 1.0 / HF
            sch.do("dve", lambda e, s0=mkS(0): e.tensor_single_scalar(
                s0, s0, inv, ALU.mult), reads=[stb], writes=[stb])
            sch.do("dve", lambda e, s2=mkS(2), s0=mkS(0): e.tensor_tensor(
                s2, s0, s0, ALU.mult), reads=[stb], writes=[stb])
            sch.do("dve", lambda e, s1=mkS(1): e.tensor_scalar(
                s1, s1, inv, 0.0, ALU.mult, ALU.add),
                reads=[stb], writes=[stb])
            sch.do("dve", lambda e, s1=mkS(1), s2=mkS(2): e.tensor_tensor(
                s1, s1, s2, ALU.subtract), reads=[stb], writes=[stb])
            sch.do("act", lambda e, s6=mkS(6), s1=mkS(1): e.activation(
                s6, s1, AF.Sqrt, bias=c(S_EPS), scale=1.0),
                reads=[stb, cb], writes=[stb])
            sch.do("dve", lambda e, s3=mkS(3), s6=mkS(6): e.reciprocal(
                s3, s6), reads=[stb], writes=[stb])
            sch.do("dve", lambda e, s6=mkS(6), s3=mkS(3): e.tensor_tensor(
                s6, s3, s3, ALU.mult), reads=[stb], writes=[stb])
            sch.do("dve", lambda e, s7=mkS(7), s1=mkS(1):
                   e.tensor_single_scalar(s7, s1, EPS, ALU.add),
                   reads=[stb], writes=[stb])
            sch.do("dve", lambda e, s6=mkS(6), s7=mkS(7): e.tensor_tensor(
                s6, s6, s7, ALU.mult), reads=[stb], writes=[stb])
            sch.do("dve", lambda e, s6=mkS(6): e.tensor_scalar(
                s6, s6, -0.5, 1.5, ALU.mult, ALU.add),
                reads=[stb], writes=[stb])
            sch.do("dve", lambda e, s3=mkS(3), s6=mkS(6): e.tensor_tensor(
                s3, s3, s6, ALU.mult), reads=[stb], writes=[stb])
            sch.do("dve", lambda e, s6=mkS(6), s0=mkS(0), s3=mkS(3):
                   e.tensor_tensor(s6, s0, s3, ALU.mult),
                   reads=[stb], writes=[stb])
            sch.do("dve", lambda e, s6=mkS(6): e.tensor_single_scalar(
                s6, s6, -1.0, ALU.mult), reads=[stb], writes=[stb])
            for hs in range(HF // 512):
                sl = slice(hs * 512, hs * 512 + 512)
                g3 = G3S[:, (hs % 2) * 512:(hs % 2) * 512 + 512]
                sch.do("act", lambda e, sl=sl, g3=g3, s6=mkS(6),
                       s3=mkS(3): e.activation(
                    g3, H3[:, sl], AF.Identity, bias=s6, scale=s3),
                    reads=[h3bs[hs], stb], writes=[g3bs[hs % 2]])
                sch.do("dve", lambda e, sl=sl, g3=g3: e.tensor_tensor(
                    H4[:, sl], g3, WBC[:, sl], ALU.mult),
                    reads=[g3bs[hs % 2], bcb], writes=[h4bs[hs]])
                sch.do("gp", lambda e, sl=sl: e.tensor_tensor(
                    H4[:, sl], H4[:, sl], BBC[:, sl], ALU.add),
                    reads=[h4bs[hs], bcb], writes=[h4bs[hs]])
                amax_partial(H4, h4bs[hs], hs, par)
            amax_scale(8, 10, 2, par)
            quant_tile(H4, h4bs, HF, K2N, k2bs, mkS(3), stb)
            pend_m2 = []
            for g8 in range(4):
                kpar = g8 % 2
                def tk2(e, g8=g8):
                    ins = []
                    for j in range(8):
                        ins.append(e.matmul(
                            PT[:, j * 128:(j + 1) * 128],
                            K2N[:, (g8 * 8 + j) * 128:
                                (g8 * 8 + j) * 128 + 128], IDBF[:],
                            start=(j == 0), stop=(j == 7),
                            is_transpose=True))
                    return ins
                sch.do("pe", tk2, reads=[k2bs[g8 * 2], k2bs[g8 * 2 + 1],
                                         idn], writes=[ptb])
                sch.do("act", lambda e, kpar=kpar: e.activation(
                    K2F[:, kpar * 1024:(kpar + 1) * 1024], PT[:, 0:1024],
                    AF.Copy), reads=[ptb], writes=[k2fbs[kpar]])
                for ds_ in range(2):
                    pb, pst = psb[2 + ds_], PS[2 + ds_]
                    def m2(e, g8=g8, ds_=ds_, pst=pst, kpar=kpar):
                        ins = []
                        for j in range(8):
                            chn = g8 * 8 + j
                            ins.append(e.matmul(
                                pst[:],
                                K2F[:, kpar * 1024 + j * 128:
                                    kpar * 1024 + (j + 1) * 128],
                                W2Q[:, chn * D + ds_ * 512:
                                         chn * D + ds_ * 512 + 512],
                                start=(chn == 0), stop=(chn == 31)))
                        return ins
                    pend_m2.append((m2, k2fbs[kpar], psb[2 + ds_]))
                if g8 >= 1:
                    f, rb, wb = pend_m2.pop(0)
                    sch.do("pe", f, reads=[rb, w2b], writes=[wb])
                    f, rb, wb = pend_m2.pop(0)
                    sch.do("pe", f, reads=[rb, w2b], writes=[wb])
            for f, rb, wb in pend_m2:
                sch.do("pe", f, reads=[rb, w2b], writes=[wb])
            for ds_ in range(2):
                pb, pst = psb[2 + ds_], PS[2 + ds_]
                sch.do("dve", lambda e, ds_=ds_, pst=pst, s10=mkS(10):
                       e.scalar_tensor_tensor(
                           XOUT[:, ds_ * 512:ds_ * 512 + 512], pst[:],
                           s10, B2BC[:, ds_ * 512:ds_ * 512 + 512],
                           ALU.mult, ALU.add),
                       reads=[pb, stb, bcb], writes=[xob])
                sch.do("dve", lambda e, ds_=ds_, xi=xi: e.tensor_tensor(
                    XOUT[:, ds_ * 512:ds_ * 512 + 512],
                    XOUT[:, ds_ * 512:ds_ * 512 + 512],
                    xi[:, ds_ * 512:ds_ * 512 + 512], ALU.add),
                    reads=[xob, xbuf], writes=[xob])
            sch.do("gdma", lambda e, it=it, seq2=seq2: e.dma_start(
                out=dst_dram[seq2, it * 128:(it + 1) * 128, :],
                in_=XOUT[:]), reads=[xob])

        NTILE = SEQ * TT
        info = prelude(0)
        for g in range(NTILE):
            body(g)
            cur = info
            if g + 1 < NTILE:
                info = prelude(g + 1)
            tail(g, cur[0], cur[1])
        es.close()
        sch.set_fence()

    if trunc == "phaseA":
        attention_phase("a", x_in, xs_a, 0, "A0_")
        sch.do("dve", lambda e: e.engine_nop(), extra=sch.fence)
        sch.emit()
        return nc
    for seq in range(SEQ):
        attention_phase("a", x_in, xs_a, seq, f"A{seq}_")
    ffn_phase(xs_a, xs_b, True, "F0_")
    for seq in range(SEQ):
        attention_phase("b", xs_b, xs_c, seq, f"B{seq}_")
    ffn_phase(xs_c, out_d, False, "F1_")
    sch.do("dve", lambda e: e.engine_nop(), extra=sch.fence)
    sch.emit()
    return nc


_NC_CACHE = [None]


def kernel(**inputs):
    x = np.ascontiguousarray(inputs["L0_attn_a_x"], dtype=np.float32)
    shards = np.split(x, NCORES, axis=0)
    wmap = {
        "aWq": "L0_attn_a_Wq", "aWk": "L0_attn_a_Wk", "aWv": "L0_attn_a_Wv",
        "aWo": "L0_attn_a_Wo", "abo": "L0_attn_a_bo",
        "bWq": "L0_attn_b_Wq", "bWk": "L0_attn_b_Wk", "bWv": "L0_attn_b_Wv",
        "bWo": "L0_attn_b_Wo",
        "fw1": "L0_ffn_w1", "fb1": "L0_ffn_b1",
        "fln1w": "L0_ffn_ln1_w", "fln1b": "L0_ffn_ln1_b",
        "fln2w": "L0_ffn_ln2_w", "fln2b": "L0_ffn_ln2_b",
        "fw2": "L0_ffn_w2", "fb2": "L0_ffn_b2",
    }
    if _NC_CACHE[0] is None:
        _NC_CACHE[0] = build_program()
    nc = _NC_CACHE[0]
    in_maps = []
    for s in shards:
        m = {"x": np.ascontiguousarray(s)}
        for k, v in wmap.items():
            m[k] = np.ascontiguousarray(inputs[v], dtype=np.float32)
        in_maps.append(m)
    res = run_bass_kernel_spmd(nc, in_maps, core_ids=list(range(NCORES)))
    return np.concatenate([r["out"] for r in res.results], axis=0)
